# version 05
# baseline (speedup 1.0000x reference)
import numpy as np
from contextlib import ExitStack

import concourse.bass as bass
import concourse.mybir as mybir
from concourse.bass_utils import run_bass_kernel_spmd

F32 = mybir.dt.float32
F16 = mybir.dt.float16
AF = mybir.ActivationFunctionType
OP = mybir.AluOpType
AX = mybir.AxisListType

D = 1024
T = 2048
NE = 16
II = 512
KC = D // 128
NTT = T // 128
BIG = 65536.0

COLOF = lambda kc: 4 * (kc % 2) + kc // 2


def build_nc(t_steps=T):
    nc = bass.Bass("TRN2", target_bir_lowering=False)

    x_fm = nc.dram_tensor("x_fm", [D, T], F32, kind="ExternalInput")
    wiT = nc.dram_tensor("wiT", [D, D], F32, kind="ExternalInput")
    whT = nc.dram_tensor("whT", [D, D], F32, kind="ExternalInput")
    b_in = nc.dram_tensor("b_in", [2, D], F32, kind="ExternalInput")
    gwT = nc.dram_tensor("gwT", [D, NE], F32, kind="ExternalInput")
    ebias = nc.dram_tensor("ebias", [1, NE], F32, kind="ExternalInput")
    gpT = nc.dram_tensor("gpT", [NE, D, II], F32, kind="ExternalInput")
    upT = nc.dram_tensor("upT", [NE, D, II], F32, kind="ExternalInput")
    dpT = nc.dram_tensor("dpT", [NE, II, D], F32, kind="ExternalInput")
    ident = nc.dram_tensor("ident", [128, 128], F16, kind="ExternalInput")
    sel4 = nc.dram_tensor("sel4", [128, 4], F16, kind="ExternalInput")
    ones16r = nc.dram_tensor("ones16r", [1, 128], F16, kind="ExternalInput")
    ones32r = nc.dram_tensor("ones32r", [1, 128], F32, kind="ExternalInput")
    invD16c = nc.dram_tensor("invD16c", [128, 1], F16, kind="ExternalInput")
    esel_in = nc.dram_tensor("esel_in", [16, NE * 128], F16, kind="ExternalInput")
    cjmb_in = nc.dram_tensor("cjmb_in", [128, 256], F32, kind="ExternalInput")
    cj_in = nc.dram_tensor("cj_in", [128, 256], F32, kind="ExternalInput")
    cgmb_in = nc.dram_tensor("cgmb_in", [128, 64], F32, kind="ExternalInput")
    cg_in = nc.dram_tensor("cg_in", [128, 64], F32, kind="ExternalInput")
    cemb_in = nc.dram_tensor("cemb_in", [128, 256], F32, kind="ExternalInput")
    ce_in = nc.dram_tensor("ce_in", [128, 256], F32, kind="ExternalInput")
    outT = nc.dram_tensor("outT", [D, T], F32, kind="ExternalOutput")

    es = ExitStack()
    with es:
        id_sb = es.enter_context(nc.sbuf_tensor("id_sb", [128, 128], F16))
        sel_sb = es.enter_context(nc.sbuf_tensor("sel_sb", [128, 4], F16))
        o16r_sb = es.enter_context(nc.sbuf_tensor("o16r_sb", [1, 128], F16))
        o32r_sb = es.enter_context(nc.sbuf_tensor("o32r_sb", [1, 128], F32))
        invD_sb = es.enter_context(nc.sbuf_tensor("invD_sb", [128, 1], F16))
        eps_sb = es.enter_context(nc.sbuf_tensor("eps_sb", [128, 1], F32))
        eb_sb = es.enter_context(nc.sbuf_tensor("eb_sb", [1, NE], F32))
        gwT16 = es.enter_context(nc.sbuf_tensor("gwT16", [128, KC * NE], F16))
        esel_sb = es.enter_context(nc.sbuf_tensor("esel_sb", [16, NE * 128], F16))
        cjmb_sb = es.enter_context(nc.sbuf_tensor("cjmb_sb", [128, 256], F32))
        cj_sb = es.enter_context(nc.sbuf_tensor("cj_sb", [128, 256], F32))
        cgmb_sb = es.enter_context(nc.sbuf_tensor("cgmb_sb", [128, 64], F32))
        cg_sb = es.enter_context(nc.sbuf_tensor("cg_sb", [128, 64], F32))
        cemb_sb = es.enter_context(nc.sbuf_tensor("cemb_sb", [128, 256], F32))
        ce_sb = es.enter_context(nc.sbuf_tensor("ce_sb", [128, 256], F32))

        def sem(name):
            return es.enter_context(nc.semaphore(name))

        s_ld1, s_lg1, s_pe1, s_dv1 = sem("ld1"), sem("lg1"), sem("pe1"), sem("dv1")
        s_mm, s_th, s_tr, s_ev = sem("mm"), sem("th"), sem("tr"), sem("ev")
        s_mm1, s_th1, s_tr1, s_ev1 = sem("mm1"), sem("th1"), sem("tr1"), sem("ev1")
        s_ld3, s_dv3, s_pe3, s_ac3 = sem("ld3"), sem("dv3"), sem("pe3"), sem("ac3")
        s_plg, s_asg, s_dvr, s_ptr = sem("plg"), sem("asg"), sem("dvr"), sem("ptr")
        s_lg4, s_pwb, s_dwb, s_pg, s_asl = sem("lg4"), sem("pwb"), sem("dwb"), sem("pg"), sem("asl")
        s_dvm, s_pd, s_dev = sem("dvm"), sem("pd"), sem("dev")
        s_dv5, s_pe5, s_ac5, s_st = sem("dv5"), sem("pe5"), sem("ac5"), sem("st")

        hold_a = ExitStack()
        xi_stream = hold_a.enter_context(nc.sbuf_tensor("xi_stream", [128, NTT * D], F16))
        whT16 = hold_a.enter_context(nc.sbuf_tensor("whT16", [128, KC * D], F16))

        with ExitStack() as p1:
            x16 = p1.enter_context(nc.sbuf_tensor("x16", [128, KC * T], F16))
            wiT16 = p1.enter_context(nc.sbuf_tensor("wiT16", [128, KC * D], F16))
            bi_sb = p1.enter_context(nc.sbuf_tensor("bi_sb", [1, D], F32))
            bh_sb = p1.enter_context(nc.sbuf_tensor("bh_sb", [1, D], F32))
            brow = p1.enter_context(nc.sbuf_tensor("brow", [1, D], F32))
            brow16 = p1.enter_context(nc.sbuf_tensor("brow16", [1, D], F16))
            bias_sb = p1.enter_context(nc.sbuf_tensor("bias_sb", [128, D], F32))
            ps = [p1.enter_context(nc.psum_tensor(f"p1ps{i}", [128, 512], F32)) for i in range(2)]

            with nc.Block() as blk:
                @blk.sync
                def _(sync):
                    loads = [(id_sb, ident), (sel_sb, sel4), (o16r_sb, ones16r),
                             (o32r_sb, ones32r), (invD_sb, invD16c), (bi_sb, b_in[0:1, :]), (bh_sb, b_in[1:2, :]),
                             (eb_sb, ebias), (esel_sb, esel_in), (cjmb_sb, cjmb_in),
                             (cj_sb, cj_in), (cgmb_sb, cgmb_in), (cg_sb, cg_in),
                             (cemb_sb, cemb_in), (ce_sb, ce_in)]
                    for dst, srcap in loads:
                        if not isinstance(srcap, bass.AP):
                            srcap = srcap[:]
                        sync.dma_start(out=dst[:], in_=srcap).then_inc(s_ld1, 16)

                @blk.gpsimd
                def _(gp):
                    gp.dma_start(out=x16[:].rearrange("p (c t) -> p c t", c=KC),
                                 in_=x_fm[:].rearrange("(c p) t -> p c t", p=128)).then_inc(s_lg1, 16)
                    gp.dma_start(out=wiT16[:].rearrange("p (c n) -> p c n", c=KC),
                                 in_=wiT[:].rearrange("(c p) n -> p c n", p=128)).then_inc(s_lg1, 16)
                    gp.dma_start(out=whT16[:].rearrange("p (c n) -> p c n", c=KC),
                                 in_=whT[:].rearrange("(c p) n -> p c n", p=128)).then_inc(s_lg1, 16)
                    gp.dma_start(out=gwT16[:].rearrange("p (c n) -> p c n", c=KC),
                                 in_=gwT[:].rearrange("(c p) n -> p c n", p=128)).then_inc(s_lg1, 16)

                @blk.vector
                def _(v):
                    v.memset(eps_sb[:], 1e-5)
                    v.wait_ge(s_ld1, 16 * 15)
                    v.tensor_tensor(out=brow[:], in0=bi_sb[0:1, :], in1=bh_sb[0:1, :], op=OP.add).then_inc(s_dv1, 1)
                    v.wait_ge(s_dv1, 1)
                    v.tensor_copy(out=brow16[:], in_=brow[:]).then_inc(s_dv1, 1)
                    for h in range(2):
                        v.wait_ge(s_pe1, 1 + h)
                        v.tensor_copy(out=bias_sb[:, 512 * h:512 * (h + 1)], in_=ps[h][:, :]).then_inc(s_dv1, 1)
                    for n in range(2 * NTT):
                        v.wait_ge(s_pe1, 3 + n)
                        m, h = n // 2, n % 2
                        v.tensor_tensor(out=xi_stream[:, m * D + 512 * h: m * D + 512 * (h + 1)],
                                        in0=ps[n % 2][:, :], in1=bias_sb[:, 512 * h:512 * (h + 1)],
                                        op=OP.add).then_inc(s_dv1, 1)

                @blk.tensor
                def _(te):
                    te.wait_ge(s_dv1, 2)
                    for h in range(2):
                        if h == 1:
                            te.wait_ge(s_pe1, 1)
                        te.matmul(out=ps[h][:, :], lhsT=o16r_sb[:, :],
                                  rhs=brow16[0:1, 512 * h:512 * (h + 1)],
                                  start=True, stop=True).then_inc(s_pe1, 1)
                    te.wait_ge(s_lg1, 64)
                    for n in range(2 * NTT):
                        m, h = n // 2, n % 2
                        te.wait_ge(s_dv1, n + 3)
                        for kc in range(KC):
                            mi = te.matmul(out=ps[n % 2][:, :],
                                           lhsT=x16[:, kc * T + 128 * m: kc * T + 128 * (m + 1)],
                                           rhs=wiT16[:, kc * D + 512 * h: kc * D + 512 * (h + 1)],
                                           start=(kc == 0), stop=(kc == KC - 1))
                        mi.then_inc(s_pe1, 1)

        hold_h = ExitStack()
        buf_h = hold_h.enter_context(nc.sbuf_tensor("buf_h", [128, KC * T], F32, side="right"))
        HFREE = KC * T

        with ExitStack() as p2:
            hsp = p2.enter_context(nc.sbuf_tensor("hsp", [128, 512], F16))
            hstat = p2.enter_context(nc.sbuf_tensor("hstat", [128, 16], F16))
            zA = [p2.enter_context(nc.psum_tensor(f"zA{i}", [128, 512], F32)) for i in range(2)]
            zB = [p2.enter_context(nc.psum_tensor(f"zB{i}", [128, 512], F32)) for i in range(2)]
            tpA = [p2.enter_context(nc.psum_tensor(f"tpA{i}", [128, 4], F32)) for i in range(2)]
            tpB = [p2.enter_context(nc.psum_tensor(f"tpB{i}", [128, 4], F32)) for i in range(2)]

            with nc.Block() as blk:
                @blk.vector
                def _(v):
                    if t_steps < T:
                        for cc in range(KC):
                            v.memset(buf_h[:, cc * T:(cc + 1) * T], 0.0)
                    for b in zA + zB:
                        v.memset(b[:], 0.0)
                    v.memset(hstat[:], 0.0).then_inc(s_ev, 1)
                    v.wait_ge(s_ev, 1)
                    v.memset(hsp[:, 0:1], 0.0).then_inc(s_ev1, 1)
                    for t in range(t_steps):
                        par = t % 2
                        hs = (1 - par) * 8
                        v.wait_ge(s_tr, t + 1)
                        v.tensor_copy(out=hstat[:, hs:hs + 4], in_=tpA[par][:, :])
                        v.tensor_copy(out=bass.AP(buf_h, t, [[HFREE, 128], [2 * T, 4]]),
                                      in_=bass.AP(tpA[par], 0, [[4, 128], [1, 4]])).then_inc(s_ev, 1)
                        v.wait_ge(s_tr1, t + 1)
                        v.tensor_copy(out=hstat[:, hs + 4:hs + 8], in_=tpB[par][:, :])
                        v.tensor_copy(out=bass.AP(buf_h, t + T, [[HFREE, 128], [2 * T, 4]]),
                                      in_=bass.AP(tpB[par], 0, [[4, 128], [1, 4]])).then_inc(s_ev1, 1)

                @blk.tensor
                def _(te):
                    halves = [(zA, 0), (zB, 128)]

                    def transp_w1(t):
                        par = t % 2
                        te.wait_ge(s_th1, t + 1)
                        for j in range(4):
                            ti = te.matmul(out=tpB[par][32 * j:32 * j + 32, 0:4],
                                           lhsT=hsp[:, par * 256 + 128 + 32 * j: par * 256 + 128 + 32 * j + 32],
                                           rhs=sel_sb[:, :], start=True, stop=True,
                                           tile_position=(0, 32 * j))
                        ti.then_inc(s_tr1, 1)

                    for t in range(t_steps):
                        par = t % 2
                        blkofs = (t // 128) * D
                        r = t % 128
                        if t >= 2:
                            te.wait_ge(s_th1, t - 1)
                        te.wait_ge(s_ev, t + 1)
                        for zb_, ofs in halves:
                            for g in range(4):
                                te.matmul(out=zb_[par][32 * g:32 * g + 32, 0:128],
                                          lhsT=bass.AP(id_sb, r, [[128, 128], [0, 32]]),
                                          rhs=xi_stream[:, blkofs + 256 * g + ofs: blkofs + 256 * g + ofs + 128],
                                          start=True, stop=False, tile_position=(0, 32 * g),
                                          skip_group_check=True)
                        for zb_, ofs in halves:
                            for kc in range(0, KC, 2):
                                for g in range(4):
                                    te.matmul(out=zb_[par][32 * g:32 * g + 32, 0:128],
                                              lhsT=bass.AP(hstat, par * 8 + COLOF(kc), [[16, 128], [0, 32]]),
                                              rhs=whT16[:, kc * D + 256 * g + ofs: kc * D + 256 * g + ofs + 128],
                                              start=False, stop=False, tile_position=(0, 32 * g),
                                              skip_group_check=True)
                        if t >= 1:
                            transp_w1(t - 1)
                        te.wait_ge(s_ev1, t + 1)
                        for zb_, ofs, s_done in [(zA, 0, s_mm), (zB, 128, s_mm1)]:
                            for kc in range(1, KC, 2):
                                for g in range(4):
                                    mi = te.matmul(out=zb_[par][32 * g:32 * g + 32, 0:128],
                                                   lhsT=bass.AP(hstat, par * 8 + COLOF(kc), [[16, 128], [0, 32]]),
                                                   rhs=whT16[:, kc * D + 256 * g + ofs: kc * D + 256 * g + ofs + 128],
                                                   start=False, stop=(kc == KC - 1), tile_position=(0, 32 * g),
                                                   skip_group_check=True)
                            mi.then_inc(s_done, 1)
                        te.wait_ge(s_th, t + 1)
                        for j in range(4):
                            ti = te.matmul(out=tpA[par][32 * j:32 * j + 32, 0:4],
                                           lhsT=hsp[:, par * 256 + 32 * j: par * 256 + 32 * j + 32],
                                           rhs=sel_sb[:, :], start=True, stop=True,
                                           tile_position=(0, 32 * j))
                        ti.then_inc(s_tr, 1)
                    transp_w1(t_steps - 1)

                @blk.scalar
                def _(sc):
                    for t in range(t_steps):
                        par = t % 2
                        sc.wait_ge(s_mm, t + 1)
                        sc.activation(out=hsp[:, par * 256:par * 256 + 128],
                                      in_=zA[par][:, 0:128], func=AF.Tanh).then_inc(s_th, 1)
                        sc.wait_ge(s_mm1, t + 1)
                        sc.activation(out=hsp[:, par * 256 + 128:par * 256 + 256],
                                      in_=zB[par][:, 0:128], func=AF.Tanh).then_inc(s_th1, 1)
        hold_a.close()

        def emit_ln(src, scr, scrv1, r16s, r2s, st_ps, mb_ps,
                    S_DV, S_PE, S_AC, pre_chunk_dve, n_pre,
                    extra_dve_tail, n_tail, sync_body=None):
            m_row = lambda n: scr[0:1, 1 * T + 512 * n: 1 * T + 512 * (n + 1)]
            s_row = lambda n: scr[0:1, 2 * T + 512 * n: 2 * T + 512 * (n + 1)]
            q_row = lambda n: scr[0:1, 3 * T + 512 * n: 3 * T + 512 * (n + 1)]
            m_rowF = scr[0:1, 1 * T:1 * T + T]
            s_rowF = scr[0:1, 2 * T:2 * T + T]
            q_rowF = scr[0:1, 3 * T:3 * T + T]

            per_c = n_pre + 2
            rows0 = KC * per_c
            rows_end = rows0 + 11

            with nc.Block() as blk:
                if sync_body is not None:
                    @blk.sync
                    def _(sync):
                        sync_body(sync)

                @blk.vector
                def _(v):
                    dvc = 0
                    for cc in range(KC):
                        if pre_chunk_dve:
                            dvc = pre_chunk_dve(v, cc, dvc)
                        if cc >= 1:
                            v.wait_ge(S_PE, cc)
                        v.wait_ge(S_DV, dvc)
                        v.tensor_copy(out=r16s[:, :], in_=src[:, cc * T:(cc + 1) * T]).then_inc(S_DV, 1)
                        dvc += 1
                        v.wait_ge(S_DV, dvc)
                        v.tensor_tensor(out=r2s[:, :], in0=src[:, cc * T:(cc + 1) * T],
                                        in1=src[:, cc * T:(cc + 1) * T], op=OP.mult).then_inc(S_DV, 1)
                        dvc += 1
                    assert dvc == rows0
                    v.wait_ge(S_PE, KC)
                    for n in range(4):
                        v.wait_ge(S_DV, dvc)
                        v.tensor_copy(out=m_row(n), in_=st_ps[n][0:1, :]).then_inc(S_DV, 1)
                        dvc += 1
                    v.wait_ge(S_DV, dvc)
                    v.tensor_tensor(out=q_rowF, in0=m_rowF, in1=m_rowF, op=OP.mult).then_inc(S_DV, 1)
                    dvc += 1
                    for n in range(4):
                        v.wait_ge(S_DV, dvc)
                        v.tensor_tensor(out=s_row(n), in0=st_ps[n][32:33, :], in1=q_row(n),
                                        op=OP.subtract).then_inc(S_DV, 1)
                        dvc += 1
                    v.wait_ge(S_AC, 4)
                    v.reciprocal(out=s_rowF, in_=s_rowF).then_inc(S_DV, 1)
                    dvc += 1
                    v.wait_ge(S_DV, dvc)
                    v.tensor_tensor(out=q_rowF, in0=m_rowF, in1=s_rowF, op=OP.mult).then_inc(S_DV, 1)
                    dvc += 1
                    assert dvc == rows_end
                    v.wait_ge(S_PE, KC + 8)
                    for cc in range(KC):
                        for n in range(4):
                            sl = slice(cc * T + 512 * n, cc * T + 512 * (n + 1))
                            v.wait_ge(S_DV, dvc)
                            v.tensor_tensor(out=scrv1[:, :], in0=src[:, sl], in1=mb_ps[n][:, :],
                                            op=OP.mult).then_inc(S_DV, 1)
                            dvc += 1
                            v.wait_ge(S_DV, dvc)
                            dvc = extra_dve_tail(v, cc, n, dvc, scrv1)

                @blk.tensor
                def _(te):
                    for cc in range(KC):
                        te.wait_ge(S_DV, n_pre * (cc + 1) + 2 * (cc + 1))
                        for n in range(4):
                            te.matmul(out=st_ps[n][0:1, :], lhsT=invD_sb[:, :],
                                      rhs=r16s[:, 512 * n:512 * (n + 1)],
                                      start=(cc == 0), stop=(cc == KC - 1), tile_position=(0, 0))
                            mi = te.matmul(out=st_ps[n][32:33, :], lhsT=invD_sb[:, :],
                                           rhs=r2s[:, 512 * n:512 * (n + 1)],
                                           start=(cc == 0), stop=(cc == KC - 1), tile_position=(0, 32))
                        mi.then_inc(S_PE, 1)
                    te.wait_ge(S_DV, rows_end)
                    k = KC
                    for n in range(4):
                        te.wait_ge(S_PE, k)
                        te.matmul(out=mb_ps[n][:, :], lhsT=o32r_sb[:, :], rhs=s_row(n),
                                  start=True, stop=True).then_inc(S_PE, 1)
                        k += 1
                        te.wait_ge(S_PE, k)
                        te.matmul(out=st_ps[n][:, :], lhsT=o32r_sb[:, :], rhs=q_row(n),
                                  start=True, stop=True).then_inc(S_PE, 1)
                        k += 1

                @blk.scalar
                def _(sc):
                    for n in range(4):
                        sc.wait_ge(S_DV, rows0 + 6 + n)
                        if n > 0:
                            sc.wait_ge(S_AC, n)
                        sc.activation(out=s_row(n), in_=s_row(n), func=AF.Sqrt,
                                      scale=1.0, bias=eps_sb[0:1, 0:1]).then_inc(S_AC, 1)

        buf_u = es.enter_context(nc.sbuf_tensor("buf_u", [128, KC * T], F32))
        hold_u16 = ExitStack()
        u16 = hold_u16.enter_context(nc.sbuf_tensor("u16", [128, KC * T], F16))
        combT16 = hold_u16.enter_context(nc.sbuf_tensor("combT16", [16, T], F16))

        with ExitStack() as p3:
            xbuf = p3.enter_context(nc.sbuf_tensor("xbuf", [128, T], F32))
            r16s = p3.enter_context(nc.sbuf_tensor("r16s3", [128, T], F16))
            r2s = p3.enter_context(nc.sbuf_tensor("r2s3", [128, T], F16))
            st_ps = [p3.enter_context(nc.psum_tensor(f"st3{i}", [128, 512], F32)) for i in range(4)]
            mb_ps = [p3.enter_context(nc.psum_tensor(f"mb3{i}", [128, 512], F32)) for i in range(4)]
            scrv1 = p3.enter_context(nc.sbuf_tensor("scrv13", [128, 512], F32))

            def pre3(v, cc, dvc):
                v.wait_ge(s_ld3, 16 * (cc + 1))
                v.wait_ge(s_dv3, dvc)
                v.tensor_tensor(out=buf_u[:, cc * T:(cc + 1) * T], in0=xbuf[:, :],
                                in1=buf_h[:, cc * T:(cc + 1) * T], op=OP.add).then_inc(s_dv3, 1)
                return dvc + 1

            def tail3(v, cc, n, dvc, scrv1_):
                sl = slice(cc * T + 512 * n, cc * T + 512 * (n + 1))
                v.tensor_tensor(out=buf_u[:, sl], in0=scrv1_[:, :], in1=st_ps[n][:, :],
                                op=OP.subtract).then_inc(s_dv3, 1)
                dvc += 1
                v.wait_ge(s_dv3, dvc)
                v.tensor_copy(out=u16[:, sl], in_=buf_u[:, sl]).then_inc(s_dv3, 1)
                return dvc + 1

            def sync3(sync):
                for cc in range(KC):
                    if cc >= 1:
                        sync.wait_ge(s_dv3, 3 * cc - 2)
                    sync.dma_start(out=xbuf[:, :], in_=x_fm[128 * cc:128 * (cc + 1), :]).then_inc(s_ld3, 16)

            emit_ln(src=buf_u, scr=buf_h, scrv1=scrv1, r16s=r16s, r2s=r2s,
                    st_ps=st_ps, mb_ps=mb_ps, S_DV=s_dv3, S_PE=s_pe3, S_AC=s_ac3,
                    pre_chunk_dve=pre3, n_pre=1, extra_dve_tail=tail3, n_tail=2,
                    sync_body=sync3)

        hold_h.close()

        with ExitStack() as p3b:
            scores = p3b.enter_context(nc.sbuf_tensor("scores", [128, 256], F32))
            combine16 = p3b.enter_context(nc.sbuf_tensor("combine16", [128, 256], F16))
            R = [p3b.enter_context(nc.sbuf_tensor(f"rtR{i}", [128, 256], F32)) for i in range(5)]
            G = [p3b.enter_context(nc.sbuf_tensor(f"rtG{i}", [128, 64], F32)) for i in range(4)]
            H = [p3b.enter_context(nc.sbuf_tensor(f"rtH{i}", [128, 16], F32)) for i in range(4)]
            lg_ps = [p3b.enter_context(nc.psum_tensor(f"lg{i}", [128, 512], F32)) for i in range(2)]
            eb_ps = p3b.enter_context(nc.psum_tensor("ebps", [128, 512], F32))
            tr_ps = [p3b.enter_context(nc.psum_tensor(f"trps{i}", [16, 512], F32)) for i in range(2)]
            rt_ops = [0]

            with nc.Block() as blk:
                @blk.vector
                def _(v):
                    dvc = [0]

                    def op(fn):
                        v.wait_ge(s_dvr, dvc[0])
                        fn().then_inc(s_dvr, 1)
                        dvc[0] += 1

                    def bc(t, inner, rep):
                        f = t.shape[1]
                        return bass.AP(t, 0, [[f, 128], [1, f], [0, rep]]) if inner == f else None

                    v.wait_ge(s_asg, NTT)
                    v.wait_ge(s_plg, 1)
                    A4 = lambda t: bass.AP(t, 0, [[256, 128], [4, 64], [1, 4]])
                    A16 = lambda t: bass.AP(t, 0, [[256, 128], [16, 16], [1, 16]])
                    B4_64 = lambda t: bass.AP(t, 0, [[64, 128], [1, 64], [0, 4]])
                    B4_16 = lambda t: bass.AP(t, 0, [[16, 128], [1, 16], [0, 4]])
                    B16 = lambda t: bass.AP(t, 0, [[16, 128], [1, 16], [0, 16]])
                    G4 = lambda t: bass.AP(t, 0, [[64, 128], [4, 16], [1, 4]])

                    op(lambda: v.tensor_tensor(out=R[0][:, :], in0=scores[:, :],
                                               in1=bass.AP(eb_ps, 0, [[512, 128], [0, 16], [1, 16]]),
                                               op=OP.add))
                    op(lambda: v.tensor_reduce(out=G[0][:, :], in_=A4(R[0]), axis=AX.X, op=OP.max))
                    op(lambda: v.tensor_tensor(out=R[1][:, :], in0=R[0][:, :], in1=B4_64(G[0]), op=OP.is_equal))
                    op(lambda: v.tensor_tensor(out=R[2][:, :], in0=R[1][:, :], in1=cjmb_sb[:, :], op=OP.mult))
                    op(lambda: v.tensor_scalar(out=R[2][:, :], in0=R[2][:, :], scalar1=BIG,
                                               scalar2=None, op0=OP.add))
                    op(lambda: v.tensor_reduce(out=G[1][:, :], in_=A4(R[2]), axis=AX.X, op=OP.min))
                    op(lambda: v.tensor_tensor(out=R[1][:, :], in0=cj_sb[:, :], in1=B4_64(G[1]), op=OP.is_equal))
                    op(lambda: v.scalar_tensor_tensor(out=R[2][:, :], in0=R[1][:, :], scalar=-BIG,
                                                      in1=R[0][:, :], op0=OP.mult, op1=OP.add))
                    op(lambda: v.tensor_reduce(out=G[1][:, :], in_=A4(R[2]), axis=AX.X, op=OP.max))
                    op(lambda: v.tensor_tensor(out=G[0][:, :], in0=G[0][:, :], in1=G[1][:, :], op=OP.add))
                    op(lambda: v.tensor_reduce(out=H[0][:, :], in_=G4(G[0]), axis=AX.X, op=OP.max))
                    op(lambda: v.tensor_tensor(out=G[1][:, :], in0=G[0][:, :], in1=B4_16(H[0]), op=OP.is_equal))
                    op(lambda: v.tensor_tensor(out=G[2][:, :], in0=G[1][:, :], in1=cgmb_sb[:, :], op=OP.mult))
                    op(lambda: v.tensor_scalar(out=G[2][:, :], in0=G[2][:, :], scalar1=BIG,
                                               scalar2=None, op0=OP.add))
                    op(lambda: v.tensor_reduce(out=H[0][:, :], in_=G4(G[2]), axis=AX.X, op=OP.min))
                    op(lambda: v.tensor_tensor(out=G[1][:, :], in0=cg_sb[:, :], in1=B4_16(H[0]), op=OP.is_equal))
                    op(lambda: v.scalar_tensor_tensor(out=G[2][:, :], in0=G[1][:, :], scalar=-BIG,
                                                      in1=G[0][:, :], op0=OP.mult, op1=OP.add))
                    op(lambda: v.tensor_reduce(out=H[0][:, :], in_=G4(G[2]), axis=AX.X, op=OP.max))
                    op(lambda: v.tensor_tensor(out=G[3][:, :], in0=G[2][:, :], in1=B4_16(H[0]), op=OP.is_equal))
                    op(lambda: v.tensor_tensor(out=G[3][:, :], in0=G[3][:, :], in1=cgmb_sb[:, :], op=OP.mult))
                    op(lambda: v.tensor_scalar(out=G[3][:, :], in0=G[3][:, :], scalar1=BIG,
                                               scalar2=None, op0=OP.add))
                    op(lambda: v.tensor_reduce(out=H[0][:, :], in_=G4(G[3]), axis=AX.X, op=OP.min))
                    op(lambda: v.tensor_tensor(out=G[2][:, :], in0=cg_sb[:, :], in1=B4_16(H[0]), op=OP.is_equal))
                    op(lambda: v.tensor_tensor(out=G[1][:, :], in0=G[1][:, :], in1=G[2][:, :], op=OP.add))
                    op(lambda: v.tensor_tensor(out=R[2][:, :], in0=R[0][:, :], in1=B4_64(G[1]), op=OP.mult))
                    for k in range(4):
                        op(lambda: v.tensor_reduce(out=H[0][:, :], in_=A16(R[2]), axis=AX.X, op=OP.max))
                        op(lambda: v.tensor_tensor(out=R[1][:, :], in0=R[2][:, :], in1=B16(H[0]), op=OP.is_equal))
                        op(lambda: v.tensor_tensor(out=R[3][:, :], in0=R[1][:, :], in1=cemb_sb[:, :], op=OP.mult))
                        op(lambda: v.tensor_scalar(out=R[3][:, :], in0=R[3][:, :], scalar1=BIG,
                                                   scalar2=None, op0=OP.add))
                        op(lambda: v.tensor_reduce(out=H[0][:, :], in_=A16(R[3]), axis=AX.X, op=OP.min))
                        op(lambda: v.tensor_tensor(out=R[1][:, :], in0=ce_sb[:, :], in1=B16(H[0]), op=OP.is_equal))
                        op(lambda: v.tensor_tensor(out=R[3][:, :], in0=R[1][:, :], in1=scores[:, :], op=OP.mult))
                        op(lambda: v.tensor_reduce(out=H[1][:, :], in_=A16(R[3]), axis=AX.X, op=OP.add))
                        if k == 0:
                            op(lambda: v.tensor_copy(out=H[2][:, :], in_=H[1][:, :]))
                            op(lambda: v.tensor_tensor(out=R[4][:, :], in0=R[1][:, :], in1=B16(H[1]), op=OP.mult))
                        else:
                            op(lambda: v.tensor_tensor(out=H[2][:, :], in0=H[2][:, :], in1=H[1][:, :], op=OP.add))
                            op(lambda: v.tensor_tensor(out=R[3][:, :], in0=R[1][:, :], in1=B16(H[1]), op=OP.mult))
                            op(lambda: v.tensor_tensor(out=R[4][:, :], in0=R[4][:, :], in1=R[3][:, :], op=OP.add))
                        op(lambda: v.scalar_tensor_tensor(out=R[2][:, :], in0=R[1][:, :], scalar=-BIG,
                                                          in1=R[2][:, :], op0=OP.mult, op1=OP.add))
                    op(lambda: v.tensor_scalar(out=H[2][:, :], in0=H[2][:, :], scalar1=1e-20,
                                               scalar2=None, op0=OP.add))
                    op(lambda: v.reciprocal(out=H[3][:, :], in_=H[2][:, :]))
                    op(lambda: v.tensor_tensor(out=R[4][:, :], in0=R[4][:, :], in1=B16(H[3]), op=OP.mult))
                    op(lambda: v.tensor_copy(out=combine16[:, :], in_=R[4][:, :]))
                    rt_ops[0] = dvc[0]
                    for rnd in range(4):
                        v.wait_ge(s_ptr, 4 * (rnd + 1))
                        v.wait_ge(s_dvr, dvc[0])
                        v.tensor_copy(out=combT16[:, 512 * rnd:512 * (rnd + 1)],
                                      in_=tr_ps[rnd % 2][:, :]).then_inc(s_dvr, 1)
                        dvc[0] += 1

                @blk.tensor
                def _(te):
                    te.matmul(out=eb_ps[:, 0:NE], lhsT=o32r_sb[:, :], rhs=eb_sb[0:1, :],
                              start=True, stop=True).then_inc(s_plg, 1)
                    for m in range(NTT):
                        if m >= 2:
                            te.wait_ge(s_asg, m - 1)
                        te.wait_ge(s_plg, m + 1)
                        for cc in range(KC):
                            mi = te.matmul(out=lg_ps[m % 2][:, 0:NE],
                                           lhsT=u16[:, cc * T + 128 * m: cc * T + 128 * (m + 1)],
                                           rhs=gwT16[:, cc * NE:(cc + 1) * NE],
                                           start=(cc == 0), stop=(cc == KC - 1))
                        mi.then_inc(s_plg, 1)
                    RT = rt_ops[0]
                    assert RT > 0
                    te.wait_ge(s_dvr, RT)
                    for q in range(NTT):
                        b, ofs = (q // 4) % 2, (q % 4) * 128
                        if q // 4 >= 2:
                            te.wait_ge(s_dvr, RT + (q // 4 - 1))
                        te.wait_ge(s_ptr, q)
                        te.matmul(out=tr_ps[b][:, ofs:ofs + 128],
                                  lhsT=combine16[:, 16 * q:16 * (q + 1)], rhs=id_sb[:, :],
                                  start=True, stop=True).then_inc(s_ptr, 1)

                @blk.scalar
                def _(sc):
                    for m in range(NTT):
                        sc.wait_ge(s_plg, m + 2)
                        if m > 0:
                            sc.wait_ge(s_asg, m)
                        sc.activation(out=scores[:, 16 * m:16 * (m + 1)], in_=lg_ps[m % 2][:, 0:NE],
                                      func=AF.Sigmoid).then_inc(s_asg, 1)

        with ExitStack() as p4:
            wg16 = [p4.enter_context(nc.sbuf_tensor(f"wg16_{i}", [128, KC * II], F16)) for i in range(2)]
            wu16 = [p4.enter_context(nc.sbuf_tensor(f"wu16_{i}", [128, KC * II], F16)) for i in range(2)]
            wd16 = [p4.enter_context(nc.sbuf_tensor(f"wd16_{i}", [128, 4 * D], F16)) for i in range(2)]
            wbc_sb = p4.enter_context(nc.sbuf_tensor("wbc_sb", [128, T], F32))
            act16 = p4.enter_context(nc.sbuf_tensor("act16", [128, 4 * 512], F16))
            s1b = [p4.enter_context(nc.sbuf_tensor(f"s1b{i}", [128, 512], F32)) for i in range(2)]
            s2b = [p4.enter_context(nc.sbuf_tensor(f"s2b{i}", [128, 512], F32)) for i in range(2)]
            ga_ps = [p4.enter_context(nc.psum_tensor(f"ga{i}", [128, 512], F32)) for i in range(2)]
            up_ps = [p4.enter_context(nc.psum_tensor(f"up{i}", [128, 512], F32)) for i in range(2)]
            dn_ps = [p4.enter_context(nc.psum_tensor(f"dn{i}", [128, 512], F32)) for i in range(2)]
            wb_ps = [p4.enter_context(nc.psum_tensor(f"wb{i}", [128, 512], F32)) for i in range(2)]

            with nc.Block() as blk:
                @blk.gpsimd
                def _(gp):
                    for e in range(NE):
                        if e >= 2:
                            gp.wait_ge(s_pd, 32 * (e - 1))
                        gp.wait_ge(s_lg4, 48 * e)
                        sl = e % 2
                        gp.dma_start(out=wg16[sl][:].rearrange("p (c i) -> p c i", c=KC),
                                     in_=gpT[e].rearrange("(c p) i -> p c i", p=128)).then_inc(s_lg4, 16)
                        gp.dma_start(out=wu16[sl][:].rearrange("p (c i) -> p c i", c=KC),
                                     in_=upT[e].rearrange("(c p) i -> p c i", p=128)).then_inc(s_lg4, 16)
                        gp.dma_start(out=wd16[sl][:].rearrange("p (c h) -> p c h", c=4),
                                     in_=dpT[e].rearrange("(c p) h -> p c h", p=128)).then_inc(s_lg4, 16)

                @blk.tensor
                def _(te):
                    for e in range(NE):
                        sl = e % 2
                        te.wait_ge(s_lg4, 48 * (e + 1))
                        te.wait_ge(s_dvm, 32 * e)
                        for tt in range(4):
                            te.wait_ge(s_dwb, 4 * e + max(0, tt - 1))
                            te.wait_ge(s_pwb, 4 * e + tt)
                            te.matmul(out=wb_ps[tt % 2][:, :],
                                      lhsT=esel_sb[:, 128 * e:128 * (e + 1)],
                                      rhs=combT16[:, 512 * tt:512 * (tt + 1)],
                                      start=True, stop=True).then_inc(s_pwb, 1)
                        for tt in range(4):
                            for ic in range(4):
                                q = 16 * e + 4 * tt + ic
                                if q >= 2:
                                    te.wait_ge(s_asl, q - 1)
                                    te.wait_ge(s_dvm, 2 * q - 3)
                                for kc in range(KC):
                                    mi = te.matmul(out=ga_ps[ic % 2][:, :],
                                                   lhsT=wg16[sl][:, kc * II + 128 * ic: kc * II + 128 * (ic + 1)],
                                                   rhs=u16[:, kc * T + 512 * tt: kc * T + 512 * (tt + 1)],
                                                   start=(kc == 0), stop=(kc == KC - 1))
                                mi.then_inc(s_pg, 1)
                                for kc in range(KC):
                                    mi = te.matmul(out=up_ps[ic % 2][:, :],
                                                   lhsT=wu16[sl][:, kc * II + 128 * ic: kc * II + 128 * (ic + 1)],
                                                   rhs=u16[:, kc * T + 512 * tt: kc * T + 512 * (tt + 1)],
                                                   start=(kc == 0), stop=(kc == KC - 1))
                                mi.then_inc(s_pg, 1)
                            base_q = 16 * e + 4 * tt
                            for hc in range(KC):
                                gidx = 32 * e + 8 * tt + hc
                                te.wait_ge(s_dvm, 2 * (base_q + 4))
                                if gidx >= 2:
                                    te.wait_ge(s_dev, gidx - 1)
                                for kci in range(4):
                                    mi = te.matmul(out=dn_ps[hc % 2][:, :],
                                                   lhsT=wd16[sl][:, kci * D + 128 * hc: kci * D + 128 * (hc + 1)],
                                                   rhs=act16[:, 512 * kci:512 * (kci + 1)],
                                                   start=(kci == 0), stop=(kci == 3))
                                mi.then_inc(s_pd, 1)

                @blk.scalar
                def _(sc):
                    for e in range(NE):
                        for tt in range(4):
                            for ic in range(4):
                                q = 16 * e + 4 * tt + ic
                                sc.wait_ge(s_pg, 2 * q + 1)
                                if q >= 2:
                                    sc.wait_ge(s_dvm, 2 * q - 3)
                                sc.activation(out=s1b[q % 2][:, :], in_=ga_ps[ic % 2][:, :],
                                              func=AF.Silu).then_inc(s_asl, 1)

                @blk.vector
                def _(v):
                    for e in range(NE):
                        for tt in range(4):
                            v.wait_ge(s_pwb, 4 * e + tt + 1)
                            v.wait_ge(s_dwb, 4 * e + tt)
                            v.tensor_copy(out=wbc_sb[:, 512 * tt:512 * (tt + 1)],
                                          in_=wb_ps[tt % 2][:, :]).then_inc(s_dwb, 1)
                        for tt in range(4):
                            for ic in range(4):
                                q = 16 * e + 4 * tt + ic
                                v.wait_ge(s_asl, q + 1)
                                v.wait_ge(s_pg, 2 * q + 2)
                                if q >= 2:
                                    v.wait_ge(s_dvm, 2 * q - 2)
                                if ic == 0:
                                    v.wait_ge(s_dwb, 4 * (e + 1))
                                    if tt >= 1:
                                        v.wait_ge(s_pd, 32 * e + 8 * tt)
                                    elif e >= 1:
                                        v.wait_ge(s_pd, 32 * e)
                                v.tensor_tensor(out=s2b[q % 2][:, :], in0=up_ps[ic % 2][:, :],
                                                in1=s1b[q % 2][:, :], op=OP.mult).then_inc(s_dvm, 1)
                                v.wait_ge(s_dvm, 2 * q + 1)
                                v.tensor_tensor(out=act16[:, 512 * ic:512 * (ic + 1)],
                                                in0=s2b[q % 2][:, :],
                                                in1=wbc_sb[:, 512 * tt:512 * (tt + 1)],
                                                op=OP.mult).then_inc(s_dvm, 1)
                            for hc in range(KC):
                                gidx = 32 * e + 8 * tt + hc
                                v.wait_ge(s_pd, gidx + 1)
                                sl4 = slice(hc * T + 512 * tt, hc * T + 512 * (tt + 1))
                                v.tensor_tensor(out=buf_u[:, sl4], in0=dn_ps[hc % 2][:, :],
                                                in1=buf_u[:, sl4], op=OP.add).then_inc(s_dev, 1)
        hold_u16.close()

        LN5_ROWS_END = KC * 2 + 11
        with ExitStack() as p5:
            scr5 = p5.enter_context(nc.sbuf_tensor("scr5", [128, 4 * T], F32))
            r16s5 = p5.enter_context(nc.sbuf_tensor("r16s5", [128, T], F16))
            r2s5 = p5.enter_context(nc.sbuf_tensor("r2s5", [128, T], F16))
            och = p5.enter_context(nc.sbuf_tensor("och", [128, 2 * T], F32))
            st_ps5 = [p5.enter_context(nc.psum_tensor(f"st5{i}", [128, 512], F32)) for i in range(4)]
            mb_ps5 = [p5.enter_context(nc.psum_tensor(f"mb5{i}", [128, 512], F32)) for i in range(4)]
            scrv15 = p5.enter_context(nc.sbuf_tensor("scrv15", [128, 512], F32))

            def tail5(v, cc, n, dvc, scrv1_):
                if cc >= 2 and n == 0:
                    v.wait_ge(s_st, 16 * (cc - 1))
                sl = slice((cc % 2) * T + 512 * n, (cc % 2) * T + 512 * (n + 1))
                v.tensor_tensor(out=och[:, sl], in0=scrv1_[:, :], in1=st_ps5[n][:, :],
                                op=OP.subtract).then_inc(s_dv5, 1)
                return dvc + 1

            def sync5(sync):
                for cc in range(KC):
                    sync.wait_ge(s_dv5, LN5_ROWS_END + 8 * (cc + 1))
                    sync.wait_ge(s_st, 16 * cc)
                    sync.dma_start(out=outT[128 * cc:128 * (cc + 1), :],
                                   in_=och[:, (cc % 2) * T:(cc % 2 + 1) * T]).then_inc(s_st, 16)
                sync.wait_ge(s_st, 16 * KC)

            emit_ln(src=buf_u, scr=scr5, scrv1=scrv15, r16s=r16s5, r2s=r2s5,
                    st_ps=st_ps5, mb_ps=mb_ps5, S_DV=s_dv5, S_PE=s_pe5, S_AC=s_ac5,
                    pre_chunk_dve=None, n_pre=0, extra_dve_tail=tail5, n_tail=1,
                    sync_body=sync5)

    return nc


def _host_consts():
    ident_np = np.eye(128, dtype=np.float16)
    sel_np = np.zeros((128, 4), np.float16)
    for n in range(4):
        sel_np[32 * n, n] = 1.0
    esel_np = np.zeros((16, NE * 128), np.float16)
    for e in range(NE):
        esel_np[e, 128 * e:128 * (e + 1)] = 1.0
    pos256 = np.arange(256)
    pos64 = np.arange(64)
    tile = lambda v: np.ascontiguousarray(np.tile(v.astype(np.float32), (128, 1)))
    return dict(ident=ident_np, sel4=sel_np,
                ones16r=np.ones((1, 128), np.float16),
                ones32r=np.ones((1, 128), np.float32),
                invD16c=np.full((128, 1), 1.0 / D, np.float16),
                esel_in=esel_np,
                cjmb_in=tile(pos256 % 4 - BIG), cj_in=tile(pos256 % 4),
                cgmb_in=tile(pos64 % 4 - BIG), cg_in=tile(pos64 % 4),
                cemb_in=tile(pos256 % 16 - BIG), ce_in=tile(pos256 % 16))


def kernel(L0_attn_x, L0_attn_Wi, L0_attn_bi, L0_attn_Wh, L0_attn_bh,
           L0_ffn_gate_weight, L0_ffn_e_bias, L0_ffn_gate_proj, L0_ffn_up_proj,
           L0_ffn_down_proj):
    x = np.asarray(L0_attn_x, np.float32)
    shared = dict(
        wiT=np.ascontiguousarray(np.asarray(L0_attn_Wi, np.float32).T),
        whT=np.ascontiguousarray(np.asarray(L0_attn_Wh, np.float32).T),
        b_in=np.ascontiguousarray(np.stack([np.asarray(L0_attn_bi, np.float32),
                                            np.asarray(L0_attn_bh, np.float32)])),
        gwT=np.ascontiguousarray(np.asarray(L0_ffn_gate_weight, np.float32).T),
        ebias=np.ascontiguousarray(np.asarray(L0_ffn_e_bias, np.float32)[None, :]),
        gpT=np.ascontiguousarray(np.asarray(L0_ffn_gate_proj, np.float32).transpose(0, 2, 1)),
        upT=np.ascontiguousarray(np.asarray(L0_ffn_up_proj, np.float32).transpose(0, 2, 1)),
        dpT=np.ascontiguousarray(np.asarray(L0_ffn_down_proj, np.float32).transpose(0, 2, 1)),
        **_host_consts(),
    )
    try:
        from concourse.compiler_utils import get_compiler_flags, set_compiler_flags
        set_compiler_flags([f.replace("--enable-ldw-opt=false", "--enable-ldw-opt=true")
                            for f in get_compiler_flags()])
    except Exception:
        pass
    nc = build_nc()
    in_maps = [dict(x_fm=np.ascontiguousarray(x[c].T), **shared) for c in range(8)]
    res = run_bass_kernel_spmd(nc, in_maps, core_ids=list(range(8)))
    out = np.stack([res.results[c]["outT"] for c in range(8)])
    return np.ascontiguousarray(out.transpose(0, 2, 1))


# version 06
# speedup vs baseline: 1.1421x; 1.1421x over previous
import numpy as np
from contextlib import ExitStack

import concourse.bass as bass
import concourse.mybir as mybir
from concourse.bass_utils import run_bass_kernel_spmd

F32 = mybir.dt.float32
F16 = mybir.dt.float16
AF = mybir.ActivationFunctionType
OP = mybir.AluOpType
AX = mybir.AxisListType

D = 1024
T = 2048
NE = 16
II = 512
KC = D // 128
NTT = T // 128
BIG = 65536.0

COLOF = lambda kc: 4 * (kc % 2) + kc // 2


def build_nc(t_steps=T):
    nc = bass.Bass("TRN2", target_bir_lowering=False)

    x_fm = nc.dram_tensor("x_fm", [D, T], F32, kind="ExternalInput")
    wiT = nc.dram_tensor("wiT", [D, D], F32, kind="ExternalInput")
    whT = nc.dram_tensor("whT", [D, D], F32, kind="ExternalInput")
    b_in = nc.dram_tensor("b_in", [2, D], F32, kind="ExternalInput")
    gwT = nc.dram_tensor("gwT", [D, NE], F32, kind="ExternalInput")
    ebias = nc.dram_tensor("ebias", [1, NE], F32, kind="ExternalInput")
    gpT = nc.dram_tensor("gpT", [NE, D, II], F32, kind="ExternalInput")
    upT = nc.dram_tensor("upT", [NE, D, II], F32, kind="ExternalInput")
    dpT = nc.dram_tensor("dpT", [NE, II, D], F32, kind="ExternalInput")
    ident = nc.dram_tensor("ident", [128, 128], F16, kind="ExternalInput")
    sel4 = nc.dram_tensor("sel4", [128, 4], F16, kind="ExternalInput")
    ones16r = nc.dram_tensor("ones16r", [1, 128], F16, kind="ExternalInput")
    ones32r = nc.dram_tensor("ones32r", [1, 128], F32, kind="ExternalInput")
    invD16c = nc.dram_tensor("invD16c", [128, 1], F16, kind="ExternalInput")
    esel_in = nc.dram_tensor("esel_in", [16, NE * 128], F16, kind="ExternalInput")
    cjmb_in = nc.dram_tensor("cjmb_in", [128, 256], F32, kind="ExternalInput")
    cj_in = nc.dram_tensor("cj_in", [128, 256], F32, kind="ExternalInput")
    cgmb_in = nc.dram_tensor("cgmb_in", [128, 64], F32, kind="ExternalInput")
    cg_in = nc.dram_tensor("cg_in", [128, 64], F32, kind="ExternalInput")
    cemb_in = nc.dram_tensor("cemb_in", [128, 256], F32, kind="ExternalInput")
    ce_in = nc.dram_tensor("ce_in", [128, 256], F32, kind="ExternalInput")
    outT = nc.dram_tensor("outT", [D, T], F32, kind="ExternalOutput")

    es = ExitStack()
    with es:
        id_sb = es.enter_context(nc.sbuf_tensor("id_sb", [128, 128], F16))
        sel_sb = es.enter_context(nc.sbuf_tensor("sel_sb", [128, 4], F16))
        o16r_sb = es.enter_context(nc.sbuf_tensor("o16r_sb", [1, 128], F16))
        o32r_sb = es.enter_context(nc.sbuf_tensor("o32r_sb", [1, 128], F32))
        invD_sb = es.enter_context(nc.sbuf_tensor("invD_sb", [128, 1], F16))
        eps_sb = es.enter_context(nc.sbuf_tensor("eps_sb", [128, 1], F32))
        eb_sb = es.enter_context(nc.sbuf_tensor("eb_sb", [1, NE], F32))
        gwT16 = es.enter_context(nc.sbuf_tensor("gwT16", [128, KC * NE], F16))
        esel_sb = es.enter_context(nc.sbuf_tensor("esel_sb", [16, NE * 128], F16))
        cjmb_sb = es.enter_context(nc.sbuf_tensor("cjmb_sb", [128, 256], F32))
        cj_sb = es.enter_context(nc.sbuf_tensor("cj_sb", [128, 256], F32))
        cgmb_sb = es.enter_context(nc.sbuf_tensor("cgmb_sb", [128, 64], F32))
        cg_sb = es.enter_context(nc.sbuf_tensor("cg_sb", [128, 64], F32))
        cemb_sb = es.enter_context(nc.sbuf_tensor("cemb_sb", [128, 256], F32))
        ce_sb = es.enter_context(nc.sbuf_tensor("ce_sb", [128, 256], F32))

        def sem(name):
            return es.enter_context(nc.semaphore(name))

        s_ld1, s_lg1, s_pe1, s_dv1 = sem("ld1"), sem("lg1"), sem("pe1"), sem("dv1")
        s_mm, s_th, s_tr, s_ev = sem("mm"), sem("th"), sem("tr"), sem("ev")
        s_mm1, s_th1, s_tr1, s_ev1 = sem("mm1"), sem("th1"), sem("tr1"), sem("ev1")
        s_ld3, s_dv3, s_pe3, s_ac3 = sem("ld3"), sem("dv3"), sem("pe3"), sem("ac3")
        s_plg, s_asg, s_dvr, s_ptr = sem("plg"), sem("asg"), sem("dvr"), sem("ptr")
        s_lg4, s_pwb, s_dwb, s_pg, s_asl = sem("lg4"), sem("pwb"), sem("dwb"), sem("pg"), sem("asl")
        s_dvm, s_pd, s_dev = sem("dvm"), sem("pd"), sem("dev")
        s_dv5, s_pe5, s_ac5, s_st = sem("dv5"), sem("pe5"), sem("ac5"), sem("st")

        hold_a = ExitStack()
        xi_stream = hold_a.enter_context(nc.sbuf_tensor("xi_stream", [128, NTT * D], F16))
        whT16 = hold_a.enter_context(nc.sbuf_tensor("whT16", [128, KC * D], F16))

        with ExitStack() as p1:
            x16 = p1.enter_context(nc.sbuf_tensor("x16", [128, KC * T], F16))
            wiT16 = p1.enter_context(nc.sbuf_tensor("wiT16", [128, KC * D], F16))
            bi_sb = p1.enter_context(nc.sbuf_tensor("bi_sb", [1, D], F32))
            bh_sb = p1.enter_context(nc.sbuf_tensor("bh_sb", [1, D], F32))
            brow = p1.enter_context(nc.sbuf_tensor("brow", [1, D], F32))
            brow16 = p1.enter_context(nc.sbuf_tensor("brow16", [1, D], F16))
            bias_sb = p1.enter_context(nc.sbuf_tensor("bias_sb", [128, D], F32))
            ps = [p1.enter_context(nc.psum_tensor(f"p1ps{i}", [128, 512], F32)) for i in range(2)]

            with nc.Block() as blk:
                @blk.sync
                def _(sync):
                    loads = [(id_sb, ident), (sel_sb, sel4), (o16r_sb, ones16r),
                             (o32r_sb, ones32r), (invD_sb, invD16c), (bi_sb, b_in[0:1, :]), (bh_sb, b_in[1:2, :]),
                             (eb_sb, ebias), (esel_sb, esel_in), (cjmb_sb, cjmb_in),
                             (cj_sb, cj_in), (cgmb_sb, cgmb_in), (cg_sb, cg_in),
                             (cemb_sb, cemb_in), (ce_sb, ce_in)]
                    for dst, srcap in loads:
                        if not isinstance(srcap, bass.AP):
                            srcap = srcap[:]
                        sync.dma_start(out=dst[:], in_=srcap).then_inc(s_ld1, 16)

                @blk.gpsimd
                def _(gp):
                    gp.dma_start(out=x16[:].rearrange("p (c t) -> p c t", c=KC),
                                 in_=x_fm[:].rearrange("(c p) t -> p c t", p=128)).then_inc(s_lg1, 16)
                    gp.dma_start(out=wiT16[:].rearrange("p (c n) -> p c n", c=KC),
                                 in_=wiT[:].rearrange("(c p) n -> p c n", p=128)).then_inc(s_lg1, 16)
                    gp.dma_start(out=whT16[:].rearrange("p (c n) -> p c n", c=KC),
                                 in_=whT[:].rearrange("(c p) n -> p c n", p=128)).then_inc(s_lg1, 16)
                    gp.dma_start(out=gwT16[:].rearrange("p (c n) -> p c n", c=KC),
                                 in_=gwT[:].rearrange("(c p) n -> p c n", p=128)).then_inc(s_lg1, 16)

                @blk.vector
                def _(v):
                    v.memset(eps_sb[:], 1e-5)
                    v.wait_ge(s_ld1, 16 * 15)
                    v.tensor_tensor(out=brow[:], in0=bi_sb[0:1, :], in1=bh_sb[0:1, :], op=OP.add).then_inc(s_dv1, 1)
                    v.wait_ge(s_dv1, 1)
                    v.tensor_copy(out=brow16[:], in_=brow[:]).then_inc(s_dv1, 1)
                    for h in range(2):
                        v.wait_ge(s_pe1, 1 + h)
                        v.tensor_copy(out=bias_sb[:, 512 * h:512 * (h + 1)], in_=ps[h][:, :]).then_inc(s_dv1, 1)
                    for n in range(2 * NTT):
                        v.wait_ge(s_pe1, 3 + n)
                        m, h = n // 2, n % 2
                        v.tensor_tensor(out=xi_stream[:, m * D + 512 * h: m * D + 512 * (h + 1)],
                                        in0=ps[n % 2][:, :], in1=bias_sb[:, 512 * h:512 * (h + 1)],
                                        op=OP.add).then_inc(s_dv1, 1)

                @blk.tensor
                def _(te):
                    te.wait_ge(s_dv1, 2)
                    for h in range(2):
                        if h == 1:
                            te.wait_ge(s_pe1, 1)
                        te.matmul(out=ps[h][:, :], lhsT=o16r_sb[:, :],
                                  rhs=brow16[0:1, 512 * h:512 * (h + 1)],
                                  start=True, stop=True).then_inc(s_pe1, 1)
                    te.wait_ge(s_lg1, 64)
                    for n in range(2 * NTT):
                        m, h = n // 2, n % 2
                        te.wait_ge(s_dv1, n + 3)
                        for kc in range(KC):
                            mi = te.matmul(out=ps[n % 2][:, :],
                                           lhsT=x16[:, kc * T + 128 * m: kc * T + 128 * (m + 1)],
                                           rhs=wiT16[:, kc * D + 512 * h: kc * D + 512 * (h + 1)],
                                           start=(kc == 0), stop=(kc == KC - 1))
                        mi.then_inc(s_pe1, 1)

        hold_h = ExitStack()
        buf_h = hold_h.enter_context(nc.sbuf_tensor("buf_h", [128, KC * T], F32, side="right"))
        HFREE = KC * T

        with ExitStack() as p2:
            hsp = p2.enter_context(nc.sbuf_tensor("hsp", [128, 512], F16))
            hstat = p2.enter_context(nc.sbuf_tensor("hstat", [128, 16], F16))
            zA = [p2.enter_context(nc.psum_tensor(f"zA{i}", [128, 512], F32)) for i in range(2)]
            zB = [p2.enter_context(nc.psum_tensor(f"zB{i}", [128, 512], F32)) for i in range(2)]
            tpA = [p2.enter_context(nc.psum_tensor(f"tpA{i}", [128, 4], F32)) for i in range(2)]
            tpB = [p2.enter_context(nc.psum_tensor(f"tpB{i}", [128, 4], F32)) for i in range(2)]

            with nc.Block() as blk:
                @blk.vector
                def _(v):
                    if t_steps < T:
                        for cc in range(KC):
                            v.memset(buf_h[:, cc * T:(cc + 1) * T], 0.0)
                    for b in zA + zB:
                        v.memset(b[:], 0.0)
                    v.memset(hstat[:], 0.0).then_inc(s_ev, 1)
                    v.wait_ge(s_ev, 1)
                    v.memset(hsp[:, 0:1], 0.0).then_inc(s_ev1, 1)
                    for t in range(t_steps):
                        par = t % 2
                        hs = (1 - par) * 8
                        v.wait_ge(s_tr, t + 1)
                        v.tensor_copy(out=hstat[:, hs:hs + 4], in_=tpA[par][:, :])
                        v.tensor_copy(out=bass.AP(buf_h, t, [[HFREE, 128], [2 * T, 4]]),
                                      in_=bass.AP(tpA[par], 0, [[4, 128], [1, 4]])).then_inc(s_ev, 1)
                        v.wait_ge(s_tr1, t + 1)
                        v.tensor_copy(out=hstat[:, hs + 4:hs + 8], in_=tpB[par][:, :])
                        v.tensor_copy(out=bass.AP(buf_h, t + T, [[HFREE, 128], [2 * T, 4]]),
                                      in_=bass.AP(tpB[par], 0, [[4, 128], [1, 4]])).then_inc(s_ev1, 1)

                @blk.tensor
                def _(te):
                    def xi_seed(t, zb_, ofs):
                        blkofs = (t // 128) * D
                        r = t % 128
                        par = t % 2
                        for g in range(4):
                            te.matmul(out=zb_[par][32 * g:32 * g + 32, 0:128],
                                      lhsT=bass.AP(id_sb, r, [[128, 128], [0, 32]]),
                                      rhs=xi_stream[:, blkofs + 256 * g + ofs: blkofs + 256 * g + ofs + 128],
                                      start=True, stop=False, tile_position=(0, 32 * g),
                                      skip_group_check=True)

                    def z_rounds(t, zb_, ofs, kcs, stop_last, s_done):
                        par = t % 2
                        mi = None
                        for kc in kcs:
                            for g in range(4):
                                mi = te.matmul(out=zb_[par][32 * g:32 * g + 32, 0:128],
                                               lhsT=bass.AP(hstat, par * 8 + COLOF(kc), [[16, 128], [0, 32]]),
                                               rhs=whT16[:, kc * D + 256 * g + ofs: kc * D + 256 * g + ofs + 128],
                                               start=False, stop=(stop_last and kc == kcs[-1]),
                                               tile_position=(0, 32 * g), skip_group_check=True)
                        if s_done is not None:
                            mi.then_inc(s_done, 1)

                    def transp(t, tp_, hofs, s_done):
                        par = t % 2
                        for j in range(4):
                            ti = te.matmul(out=tp_[par][32 * j:32 * j + 32, 0:4],
                                           lhsT=hsp[:, par * 256 + hofs + 32 * j: par * 256 + hofs + 32 * j + 32],
                                           rhs=sel_sb[:, :], start=True, stop=True,
                                           tile_position=(0, 32 * j))
                        ti.then_inc(s_done, 1)

                    evens = list(range(0, KC, 2))
                    odds = list(range(1, KC, 2))
                    for t in range(t_steps):
                        te.wait_ge(s_th, t)
                        te.wait_ge(s_ev, t + 1)
                        xi_seed(t, zA, 0)
                        te.wait_ge(s_th1, t)
                        xi_seed(t, zB, 128)
                        if t >= 1:
                            transp(t - 1, tpB, 128, s_tr1)
                        z_rounds(t, zA, 0, evens, False, None)
                        z_rounds(t, zB, 128, evens, False, None)
                        te.wait_ge(s_ev1, t + 1)
                        z_rounds(t, zA, 0, odds, True, s_mm)
                        z_rounds(t, zB, 128, odds, True, s_mm1)
                        te.wait_ge(s_th, t + 1)
                        transp(t, tpA, 0, s_tr)
                    te.wait_ge(s_th1, t_steps)
                    transp(t_steps - 1, tpB, 128, s_tr1)

                @blk.scalar
                def _(sc):
                    for t in range(t_steps):
                        par = t % 2
                        sc.wait_ge(s_mm, t + 1)
                        if t >= 2:
                            sc.wait_ge(s_tr, t - 1)
                        sc.activation(out=hsp[:, par * 256:par * 256 + 128],
                                      in_=zA[par][:, 0:128], func=AF.Tanh).then_inc(s_th, 1)
                        sc.wait_ge(s_mm1, t + 1)
                        if t >= 2:
                            sc.wait_ge(s_tr1, t - 1)
                        sc.activation(out=hsp[:, par * 256 + 128:par * 256 + 256],
                                      in_=zB[par][:, 0:128], func=AF.Tanh).then_inc(s_th1, 1)
        hold_a.close()

        def emit_ln(src, scr, scrv1, r16s, r2s, st_ps, mb_ps,
                    S_DV, S_PE, S_AC, pre_chunk_dve, n_pre,
                    extra_dve_tail, n_tail, sync_body=None):
            m_row = lambda n: scr[0:1, 1 * T + 512 * n: 1 * T + 512 * (n + 1)]
            s_row = lambda n: scr[0:1, 2 * T + 512 * n: 2 * T + 512 * (n + 1)]
            q_row = lambda n: scr[0:1, 3 * T + 512 * n: 3 * T + 512 * (n + 1)]
            m_rowF = scr[0:1, 1 * T:1 * T + T]
            s_rowF = scr[0:1, 2 * T:2 * T + T]
            q_rowF = scr[0:1, 3 * T:3 * T + T]

            per_c = n_pre + 2
            rows0 = KC * per_c
            rows_end = rows0 + 11

            with nc.Block() as blk:
                if sync_body is not None:
                    @blk.sync
                    def _(sync):
                        sync_body(sync)

                @blk.vector
                def _(v):
                    dvc = 0
                    for cc in range(KC):
                        if pre_chunk_dve:
                            dvc = pre_chunk_dve(v, cc, dvc)
                        if cc >= 1:
                            v.wait_ge(S_PE, cc)
                        v.wait_ge(S_DV, dvc)
                        v.tensor_copy(out=r16s[:, :], in_=src[:, cc * T:(cc + 1) * T]).then_inc(S_DV, 1)
                        dvc += 1
                        v.wait_ge(S_DV, dvc)
                        v.tensor_tensor(out=r2s[:, :], in0=src[:, cc * T:(cc + 1) * T],
                                        in1=src[:, cc * T:(cc + 1) * T], op=OP.mult).then_inc(S_DV, 1)
                        dvc += 1
                    assert dvc == rows0
                    v.wait_ge(S_PE, KC)
                    for n in range(4):
                        v.wait_ge(S_DV, dvc)
                        v.tensor_copy(out=m_row(n), in_=st_ps[n][0:1, :]).then_inc(S_DV, 1)
                        dvc += 1
                    v.wait_ge(S_DV, dvc)
                    v.tensor_tensor(out=q_rowF, in0=m_rowF, in1=m_rowF, op=OP.mult).then_inc(S_DV, 1)
                    dvc += 1
                    for n in range(4):
                        v.wait_ge(S_DV, dvc)
                        v.tensor_tensor(out=s_row(n), in0=st_ps[n][32:33, :], in1=q_row(n),
                                        op=OP.subtract).then_inc(S_DV, 1)
                        dvc += 1
                    v.wait_ge(S_AC, 4)
                    v.reciprocal(out=s_rowF, in_=s_rowF).then_inc(S_DV, 1)
                    dvc += 1
                    v.wait_ge(S_DV, dvc)
                    v.tensor_tensor(out=q_rowF, in0=m_rowF, in1=s_rowF, op=OP.mult).then_inc(S_DV, 1)
                    dvc += 1
                    assert dvc == rows_end
                    v.wait_ge(S_PE, KC + 8)
                    for cc in range(KC):
                        for n in range(4):
                            sl = slice(cc * T + 512 * n, cc * T + 512 * (n + 1))
                            v.wait_ge(S_DV, dvc)
                            v.tensor_tensor(out=scrv1[:, :], in0=src[:, sl], in1=mb_ps[n][:, :],
                                            op=OP.mult).then_inc(S_DV, 1)
                            dvc += 1
                            v.wait_ge(S_DV, dvc)
                            dvc = extra_dve_tail(v, cc, n, dvc, scrv1)

                @blk.tensor
                def _(te):
                    for cc in range(KC):
                        te.wait_ge(S_DV, n_pre * (cc + 1) + 2 * (cc + 1))
                        for n in range(4):
                            te.matmul(out=st_ps[n][0:1, :], lhsT=invD_sb[:, :],
                                      rhs=r16s[:, 512 * n:512 * (n + 1)],
                                      start=(cc == 0), stop=(cc == KC - 1), tile_position=(0, 0))
                            mi = te.matmul(out=st_ps[n][32:33, :], lhsT=invD_sb[:, :],
                                           rhs=r2s[:, 512 * n:512 * (n + 1)],
                                           start=(cc == 0), stop=(cc == KC - 1), tile_position=(0, 32))
                        mi.then_inc(S_PE, 1)
                    te.wait_ge(S_DV, rows_end)
                    k = KC
                    for n in range(4):
                        te.wait_ge(S_PE, k)
                        te.matmul(out=mb_ps[n][:, :], lhsT=o32r_sb[:, :], rhs=s_row(n),
                                  start=True, stop=True).then_inc(S_PE, 1)
                        k += 1
                        te.wait_ge(S_PE, k)
                        te.matmul(out=st_ps[n][:, :], lhsT=o32r_sb[:, :], rhs=q_row(n),
                                  start=True, stop=True).then_inc(S_PE, 1)
                        k += 1

                @blk.scalar
                def _(sc):
                    for n in range(4):
                        sc.wait_ge(S_DV, rows0 + 6 + n)
                        if n > 0:
                            sc.wait_ge(S_AC, n)
                        sc.activation(out=s_row(n), in_=s_row(n), func=AF.Sqrt,
                                      scale=1.0, bias=eps_sb[0:1, 0:1]).then_inc(S_AC, 1)

        buf_u = es.enter_context(nc.sbuf_tensor("buf_u", [128, KC * T], F32))
        hold_u16 = ExitStack()
        u16 = hold_u16.enter_context(nc.sbuf_tensor("u16", [128, KC * T], F16))
        combT16 = hold_u16.enter_context(nc.sbuf_tensor("combT16", [16, T], F16))

        with ExitStack() as p3:
            xbuf = p3.enter_context(nc.sbuf_tensor("xbuf", [128, T], F32))
            r16s = p3.enter_context(nc.sbuf_tensor("r16s3", [128, T], F16))
            r2s = p3.enter_context(nc.sbuf_tensor("r2s3", [128, T], F16))
            st_ps = [p3.enter_context(nc.psum_tensor(f"st3{i}", [128, 512], F32)) for i in range(4)]
            mb_ps = [p3.enter_context(nc.psum_tensor(f"mb3{i}", [128, 512], F32)) for i in range(4)]
            scrv1 = p3.enter_context(nc.sbuf_tensor("scrv13", [128, 512], F32))

            def pre3(v, cc, dvc):
                v.wait_ge(s_ld3, 16 * (cc + 1))
                v.wait_ge(s_dv3, dvc)
                v.tensor_tensor(out=buf_u[:, cc * T:(cc + 1) * T], in0=xbuf[:, :],
                                in1=buf_h[:, cc * T:(cc + 1) * T], op=OP.add).then_inc(s_dv3, 1)
                return dvc + 1

            def tail3(v, cc, n, dvc, scrv1_):
                sl = slice(cc * T + 512 * n, cc * T + 512 * (n + 1))
                v.tensor_tensor(out=buf_u[:, sl], in0=scrv1_[:, :], in1=st_ps[n][:, :],
                                op=OP.subtract).then_inc(s_dv3, 1)
                dvc += 1
                v.wait_ge(s_dv3, dvc)
                v.tensor_copy(out=u16[:, sl], in_=buf_u[:, sl]).then_inc(s_dv3, 1)
                return dvc + 1

            def sync3(sync):
                for cc in range(KC):
                    if cc >= 1:
                        sync.wait_ge(s_dv3, 3 * cc - 2)
                    sync.dma_start(out=xbuf[:, :], in_=x_fm[128 * cc:128 * (cc + 1), :]).then_inc(s_ld3, 16)

            emit_ln(src=buf_u, scr=buf_h, scrv1=scrv1, r16s=r16s, r2s=r2s,
                    st_ps=st_ps, mb_ps=mb_ps, S_DV=s_dv3, S_PE=s_pe3, S_AC=s_ac3,
                    pre_chunk_dve=pre3, n_pre=1, extra_dve_tail=tail3, n_tail=2,
                    sync_body=sync3)

        hold_h.close()

        with ExitStack() as p3b:
            scores = p3b.enter_context(nc.sbuf_tensor("scores", [128, 256], F32))
            combine16 = p3b.enter_context(nc.sbuf_tensor("combine16", [128, 256], F16))
            R = [p3b.enter_context(nc.sbuf_tensor(f"rtR{i}", [128, 256], F32)) for i in range(5)]
            G = [p3b.enter_context(nc.sbuf_tensor(f"rtG{i}", [128, 64], F32)) for i in range(4)]
            H = [p3b.enter_context(nc.sbuf_tensor(f"rtH{i}", [128, 16], F32)) for i in range(4)]
            lg_ps = [p3b.enter_context(nc.psum_tensor(f"lg{i}", [128, 512], F32)) for i in range(2)]
            eb_ps = p3b.enter_context(nc.psum_tensor("ebps", [128, 512], F32))
            tr_ps = [p3b.enter_context(nc.psum_tensor(f"trps{i}", [16, 512], F32)) for i in range(2)]
            rt_ops = [0]

            with nc.Block() as blk:
                @blk.vector
                def _(v):
                    dvc = [0]

                    def op(fn):
                        v.wait_ge(s_dvr, dvc[0])
                        fn().then_inc(s_dvr, 1)
                        dvc[0] += 1

                    def bc(t, inner, rep):
                        f = t.shape[1]
                        return bass.AP(t, 0, [[f, 128], [1, f], [0, rep]]) if inner == f else None

                    v.wait_ge(s_asg, NTT)
                    v.wait_ge(s_plg, 1)
                    A4 = lambda t: bass.AP(t, 0, [[256, 128], [4, 64], [1, 4]])
                    A16 = lambda t: bass.AP(t, 0, [[256, 128], [16, 16], [1, 16]])
                    B4_64 = lambda t: bass.AP(t, 0, [[64, 128], [1, 64], [0, 4]])
                    B4_16 = lambda t: bass.AP(t, 0, [[16, 128], [1, 16], [0, 4]])
                    B16 = lambda t: bass.AP(t, 0, [[16, 128], [1, 16], [0, 16]])
                    G4 = lambda t: bass.AP(t, 0, [[64, 128], [4, 16], [1, 4]])

                    op(lambda: v.tensor_tensor(out=R[0][:, :], in0=scores[:, :],
                                               in1=bass.AP(eb_ps, 0, [[512, 128], [0, 16], [1, 16]]),
                                               op=OP.add))
                    op(lambda: v.tensor_reduce(out=G[0][:, :], in_=A4(R[0]), axis=AX.X, op=OP.max))
                    op(lambda: v.tensor_tensor(out=R[1][:, :], in0=R[0][:, :], in1=B4_64(G[0]), op=OP.is_equal))
                    op(lambda: v.tensor_tensor(out=R[2][:, :], in0=R[1][:, :], in1=cjmb_sb[:, :], op=OP.mult))
                    op(lambda: v.tensor_scalar(out=R[2][:, :], in0=R[2][:, :], scalar1=BIG,
                                               scalar2=None, op0=OP.add))
                    op(lambda: v.tensor_reduce(out=G[1][:, :], in_=A4(R[2]), axis=AX.X, op=OP.min))
                    op(lambda: v.tensor_tensor(out=R[1][:, :], in0=cj_sb[:, :], in1=B4_64(G[1]), op=OP.is_equal))
                    op(lambda: v.scalar_tensor_tensor(out=R[2][:, :], in0=R[1][:, :], scalar=-BIG,
                                                      in1=R[0][:, :], op0=OP.mult, op1=OP.add))
                    op(lambda: v.tensor_reduce(out=G[1][:, :], in_=A4(R[2]), axis=AX.X, op=OP.max))
                    op(lambda: v.tensor_tensor(out=G[0][:, :], in0=G[0][:, :], in1=G[1][:, :], op=OP.add))
                    op(lambda: v.tensor_reduce(out=H[0][:, :], in_=G4(G[0]), axis=AX.X, op=OP.max))
                    op(lambda: v.tensor_tensor(out=G[1][:, :], in0=G[0][:, :], in1=B4_16(H[0]), op=OP.is_equal))
                    op(lambda: v.tensor_tensor(out=G[2][:, :], in0=G[1][:, :], in1=cgmb_sb[:, :], op=OP.mult))
                    op(lambda: v.tensor_scalar(out=G[2][:, :], in0=G[2][:, :], scalar1=BIG,
                                               scalar2=None, op0=OP.add))
                    op(lambda: v.tensor_reduce(out=H[0][:, :], in_=G4(G[2]), axis=AX.X, op=OP.min))
                    op(lambda: v.tensor_tensor(out=G[1][:, :], in0=cg_sb[:, :], in1=B4_16(H[0]), op=OP.is_equal))
                    op(lambda: v.scalar_tensor_tensor(out=G[2][:, :], in0=G[1][:, :], scalar=-BIG,
                                                      in1=G[0][:, :], op0=OP.mult, op1=OP.add))
                    op(lambda: v.tensor_reduce(out=H[0][:, :], in_=G4(G[2]), axis=AX.X, op=OP.max))
                    op(lambda: v.tensor_tensor(out=G[3][:, :], in0=G[2][:, :], in1=B4_16(H[0]), op=OP.is_equal))
                    op(lambda: v.tensor_tensor(out=G[3][:, :], in0=G[3][:, :], in1=cgmb_sb[:, :], op=OP.mult))
                    op(lambda: v.tensor_scalar(out=G[3][:, :], in0=G[3][:, :], scalar1=BIG,
                                               scalar2=None, op0=OP.add))
                    op(lambda: v.tensor_reduce(out=H[0][:, :], in_=G4(G[3]), axis=AX.X, op=OP.min))
                    op(lambda: v.tensor_tensor(out=G[2][:, :], in0=cg_sb[:, :], in1=B4_16(H[0]), op=OP.is_equal))
                    op(lambda: v.tensor_tensor(out=G[1][:, :], in0=G[1][:, :], in1=G[2][:, :], op=OP.add))
                    op(lambda: v.tensor_tensor(out=R[2][:, :], in0=R[0][:, :], in1=B4_64(G[1]), op=OP.mult))
                    for k in range(4):
                        op(lambda: v.tensor_reduce(out=H[0][:, :], in_=A16(R[2]), axis=AX.X, op=OP.max))
                        op(lambda: v.tensor_tensor(out=R[1][:, :], in0=R[2][:, :], in1=B16(H[0]), op=OP.is_equal))
                        op(lambda: v.tensor_tensor(out=R[3][:, :], in0=R[1][:, :], in1=cemb_sb[:, :], op=OP.mult))
                        op(lambda: v.tensor_scalar(out=R[3][:, :], in0=R[3][:, :], scalar1=BIG,
                                                   scalar2=None, op0=OP.add))
                        op(lambda: v.tensor_reduce(out=H[0][:, :], in_=A16(R[3]), axis=AX.X, op=OP.min))
                        op(lambda: v.tensor_tensor(out=R[1][:, :], in0=ce_sb[:, :], in1=B16(H[0]), op=OP.is_equal))
                        op(lambda: v.tensor_tensor(out=R[3][:, :], in0=R[1][:, :], in1=scores[:, :], op=OP.mult))
                        op(lambda: v.tensor_reduce(out=H[1][:, :], in_=A16(R[3]), axis=AX.X, op=OP.add))
                        if k == 0:
                            op(lambda: v.tensor_copy(out=H[2][:, :], in_=H[1][:, :]))
                            op(lambda: v.tensor_tensor(out=R[4][:, :], in0=R[1][:, :], in1=B16(H[1]), op=OP.mult))
                        else:
                            op(lambda: v.tensor_tensor(out=H[2][:, :], in0=H[2][:, :], in1=H[1][:, :], op=OP.add))
                            op(lambda: v.tensor_tensor(out=R[3][:, :], in0=R[1][:, :], in1=B16(H[1]), op=OP.mult))
                            op(lambda: v.tensor_tensor(out=R[4][:, :], in0=R[4][:, :], in1=R[3][:, :], op=OP.add))
                        op(lambda: v.scalar_tensor_tensor(out=R[2][:, :], in0=R[1][:, :], scalar=-BIG,
                                                          in1=R[2][:, :], op0=OP.mult, op1=OP.add))
                    op(lambda: v.tensor_scalar(out=H[2][:, :], in0=H[2][:, :], scalar1=1e-20,
                                               scalar2=None, op0=OP.add))
                    op(lambda: v.reciprocal(out=H[3][:, :], in_=H[2][:, :]))
                    op(lambda: v.tensor_tensor(out=R[4][:, :], in0=R[4][:, :], in1=B16(H[3]), op=OP.mult))
                    op(lambda: v.tensor_copy(out=combine16[:, :], in_=R[4][:, :]))
                    rt_ops[0] = dvc[0]
                    for rnd in range(4):
                        v.wait_ge(s_ptr, 4 * (rnd + 1))
                        v.wait_ge(s_dvr, dvc[0])
                        v.tensor_copy(out=combT16[:, 512 * rnd:512 * (rnd + 1)],
                                      in_=tr_ps[rnd % 2][:, :]).then_inc(s_dvr, 1)
                        dvc[0] += 1

                @blk.tensor
                def _(te):
                    te.matmul(out=eb_ps[:, 0:NE], lhsT=o32r_sb[:, :], rhs=eb_sb[0:1, :],
                              start=True, stop=True).then_inc(s_plg, 1)
                    for m in range(NTT):
                        if m >= 2:
                            te.wait_ge(s_asg, m - 1)
                        te.wait_ge(s_plg, m + 1)
                        for cc in range(KC):
                            mi = te.matmul(out=lg_ps[m % 2][:, 0:NE],
                                           lhsT=u16[:, cc * T + 128 * m: cc * T + 128 * (m + 1)],
                                           rhs=gwT16[:, cc * NE:(cc + 1) * NE],
                                           start=(cc == 0), stop=(cc == KC - 1))
                        mi.then_inc(s_plg, 1)
                    RT = rt_ops[0]
                    assert RT > 0
                    te.wait_ge(s_dvr, RT)
                    for q in range(NTT):
                        b, ofs = (q // 4) % 2, (q % 4) * 128
                        if q // 4 >= 2:
                            te.wait_ge(s_dvr, RT + (q // 4 - 1))
                        te.wait_ge(s_ptr, q)
                        te.matmul(out=tr_ps[b][:, ofs:ofs + 128],
                                  lhsT=combine16[:, 16 * q:16 * (q + 1)], rhs=id_sb[:, :],
                                  start=True, stop=True).then_inc(s_ptr, 1)

                @blk.scalar
                def _(sc):
                    for m in range(NTT):
                        sc.wait_ge(s_plg, m + 2)
                        if m > 0:
                            sc.wait_ge(s_asg, m)
                        sc.activation(out=scores[:, 16 * m:16 * (m + 1)], in_=lg_ps[m % 2][:, 0:NE],
                                      func=AF.Sigmoid).then_inc(s_asg, 1)

        with ExitStack() as p4:
            wg16 = [p4.enter_context(nc.sbuf_tensor(f"wg16_{i}", [128, KC * II], F16)) for i in range(2)]
            wu16 = [p4.enter_context(nc.sbuf_tensor(f"wu16_{i}", [128, KC * II], F16)) for i in range(2)]
            wd16 = [p4.enter_context(nc.sbuf_tensor(f"wd16_{i}", [128, 4 * D], F16)) for i in range(2)]
            wbc_sb = p4.enter_context(nc.sbuf_tensor("wbc_sb", [128, T], F32))
            act16 = p4.enter_context(nc.sbuf_tensor("act16", [128, 4 * 512], F16))
            s1b = [p4.enter_context(nc.sbuf_tensor(f"s1b{i}", [128, 512], F32)) for i in range(2)]
            s2b = [p4.enter_context(nc.sbuf_tensor(f"s2b{i}", [128, 512], F32)) for i in range(2)]
            ga_ps = [p4.enter_context(nc.psum_tensor(f"ga{i}", [128, 512], F32)) for i in range(2)]
            up_ps = [p4.enter_context(nc.psum_tensor(f"up{i}", [128, 512], F32)) for i in range(2)]
            dn_ps = [p4.enter_context(nc.psum_tensor(f"dn{i}", [128, 512], F32)) for i in range(2)]
            wb_ps = [p4.enter_context(nc.psum_tensor(f"wb{i}", [128, 512], F32)) for i in range(2)]

            with nc.Block() as blk:
                @blk.gpsimd
                def _(gp):
                    for e in range(NE):
                        if e >= 2:
                            gp.wait_ge(s_pd, 32 * (e - 1))
                        gp.wait_ge(s_lg4, 48 * e)
                        sl = e % 2
                        gp.dma_start(out=wg16[sl][:].rearrange("p (c i) -> p c i", c=KC),
                                     in_=gpT[e].rearrange("(c p) i -> p c i", p=128)).then_inc(s_lg4, 16)
                        gp.dma_start(out=wu16[sl][:].rearrange("p (c i) -> p c i", c=KC),
                                     in_=upT[e].rearrange("(c p) i -> p c i", p=128)).then_inc(s_lg4, 16)
                        gp.dma_start(out=wd16[sl][:].rearrange("p (c h) -> p c h", c=4),
                                     in_=dpT[e].rearrange("(c p) h -> p c h", p=128)).then_inc(s_lg4, 16)

                @blk.tensor
                def _(te):
                    for e in range(NE):
                        sl = e % 2
                        te.wait_ge(s_lg4, 48 * (e + 1))
                        te.wait_ge(s_dvm, 32 * e)
                        for tt in range(4):
                            te.wait_ge(s_dwb, 4 * e + max(0, tt - 1))
                            te.wait_ge(s_pwb, 4 * e + tt)
                            te.matmul(out=wb_ps[tt % 2][:, :],
                                      lhsT=esel_sb[:, 128 * e:128 * (e + 1)],
                                      rhs=combT16[:, 512 * tt:512 * (tt + 1)],
                                      start=True, stop=True).then_inc(s_pwb, 1)
                        for tt in range(4):
                            for ic in range(4):
                                q = 16 * e + 4 * tt + ic
                                if q >= 2:
                                    te.wait_ge(s_asl, q - 1)
                                    te.wait_ge(s_dvm, 2 * q - 3)
                                for kc in range(KC):
                                    mi = te.matmul(out=ga_ps[ic % 2][:, :],
                                                   lhsT=wg16[sl][:, kc * II + 128 * ic: kc * II + 128 * (ic + 1)],
                                                   rhs=u16[:, kc * T + 512 * tt: kc * T + 512 * (tt + 1)],
                                                   start=(kc == 0), stop=(kc == KC - 1))
                                mi.then_inc(s_pg, 1)
                                for kc in range(KC):
                                    mi = te.matmul(out=up_ps[ic % 2][:, :],
                                                   lhsT=wu16[sl][:, kc * II + 128 * ic: kc * II + 128 * (ic + 1)],
                                                   rhs=u16[:, kc * T + 512 * tt: kc * T + 512 * (tt + 1)],
                                                   start=(kc == 0), stop=(kc == KC - 1))
                                mi.then_inc(s_pg, 1)
                            base_q = 16 * e + 4 * tt
                            for hc in range(KC):
                                gidx = 32 * e + 8 * tt + hc
                                te.wait_ge(s_dvm, 2 * (base_q + 4))
                                if gidx >= 2:
                                    te.wait_ge(s_dev, gidx - 1)
                                for kci in range(4):
                                    mi = te.matmul(out=dn_ps[hc % 2][:, :],
                                                   lhsT=wd16[sl][:, kci * D + 128 * hc: kci * D + 128 * (hc + 1)],
                                                   rhs=act16[:, 512 * kci:512 * (kci + 1)],
                                                   start=(kci == 0), stop=(kci == 3))
                                mi.then_inc(s_pd, 1)

                @blk.scalar
                def _(sc):
                    for e in range(NE):
                        for tt in range(4):
                            for ic in range(4):
                                q = 16 * e + 4 * tt + ic
                                sc.wait_ge(s_pg, 2 * q + 1)
                                if q >= 2:
                                    sc.wait_ge(s_dvm, 2 * q - 3)
                                sc.activation(out=s1b[q % 2][:, :], in_=ga_ps[ic % 2][:, :],
                                              func=AF.Silu).then_inc(s_asl, 1)

                @blk.vector
                def _(v):
                    for e in range(NE):
                        for tt in range(4):
                            v.wait_ge(s_pwb, 4 * e + tt + 1)
                            v.wait_ge(s_dwb, 4 * e + tt)
                            v.tensor_copy(out=wbc_sb[:, 512 * tt:512 * (tt + 1)],
                                          in_=wb_ps[tt % 2][:, :]).then_inc(s_dwb, 1)
                        for tt in range(4):
                            for ic in range(4):
                                q = 16 * e + 4 * tt + ic
                                v.wait_ge(s_asl, q + 1)
                                v.wait_ge(s_pg, 2 * q + 2)
                                if q >= 2:
                                    v.wait_ge(s_dvm, 2 * q - 2)
                                if ic == 0:
                                    v.wait_ge(s_dwb, 4 * (e + 1))
                                    if tt >= 1:
                                        v.wait_ge(s_pd, 32 * e + 8 * tt)
                                    elif e >= 1:
                                        v.wait_ge(s_pd, 32 * e)
                                v.tensor_tensor(out=s2b[q % 2][:, :], in0=up_ps[ic % 2][:, :],
                                                in1=s1b[q % 2][:, :], op=OP.mult).then_inc(s_dvm, 1)
                                v.wait_ge(s_dvm, 2 * q + 1)
                                v.tensor_tensor(out=act16[:, 512 * ic:512 * (ic + 1)],
                                                in0=s2b[q % 2][:, :],
                                                in1=wbc_sb[:, 512 * tt:512 * (tt + 1)],
                                                op=OP.mult).then_inc(s_dvm, 1)
                            for hc in range(KC):
                                gidx = 32 * e + 8 * tt + hc
                                v.wait_ge(s_pd, gidx + 1)
                                sl4 = slice(hc * T + 512 * tt, hc * T + 512 * (tt + 1))
                                v.tensor_tensor(out=buf_u[:, sl4], in0=dn_ps[hc % 2][:, :],
                                                in1=buf_u[:, sl4], op=OP.add).then_inc(s_dev, 1)
        hold_u16.close()

        LN5_ROWS_END = KC * 2 + 11
        with ExitStack() as p5:
            scr5 = p5.enter_context(nc.sbuf_tensor("scr5", [128, 4 * T], F32))
            r16s5 = p5.enter_context(nc.sbuf_tensor("r16s5", [128, T], F16))
            r2s5 = p5.enter_context(nc.sbuf_tensor("r2s5", [128, T], F16))
            och = p5.enter_context(nc.sbuf_tensor("och", [128, 2 * T], F32))
            st_ps5 = [p5.enter_context(nc.psum_tensor(f"st5{i}", [128, 512], F32)) for i in range(4)]
            mb_ps5 = [p5.enter_context(nc.psum_tensor(f"mb5{i}", [128, 512], F32)) for i in range(4)]
            scrv15 = p5.enter_context(nc.sbuf_tensor("scrv15", [128, 512], F32))

            def tail5(v, cc, n, dvc, scrv1_):
                if cc >= 2 and n == 0:
                    v.wait_ge(s_st, 16 * (cc - 1))
                sl = slice((cc % 2) * T + 512 * n, (cc % 2) * T + 512 * (n + 1))
                v.tensor_tensor(out=och[:, sl], in0=scrv1_[:, :], in1=st_ps5[n][:, :],
                                op=OP.subtract).then_inc(s_dv5, 1)
                return dvc + 1

            def sync5(sync):
                for cc in range(KC):
                    sync.wait_ge(s_dv5, LN5_ROWS_END + 8 * (cc + 1))
                    sync.wait_ge(s_st, 16 * cc)
                    sync.dma_start(out=outT[128 * cc:128 * (cc + 1), :],
                                   in_=och[:, (cc % 2) * T:(cc % 2 + 1) * T]).then_inc(s_st, 16)
                sync.wait_ge(s_st, 16 * KC)

            emit_ln(src=buf_u, scr=scr5, scrv1=scrv15, r16s=r16s5, r2s=r2s5,
                    st_ps=st_ps5, mb_ps=mb_ps5, S_DV=s_dv5, S_PE=s_pe5, S_AC=s_ac5,
                    pre_chunk_dve=None, n_pre=0, extra_dve_tail=tail5, n_tail=1,
                    sync_body=sync5)

    return nc


def _host_consts():
    ident_np = np.eye(128, dtype=np.float16)
    sel_np = np.zeros((128, 4), np.float16)
    for n in range(4):
        sel_np[32 * n, n] = 1.0
    esel_np = np.zeros((16, NE * 128), np.float16)
    for e in range(NE):
        esel_np[e, 128 * e:128 * (e + 1)] = 1.0
    pos256 = np.arange(256)
    pos64 = np.arange(64)
    tile = lambda v: np.ascontiguousarray(np.tile(v.astype(np.float32), (128, 1)))
    return dict(ident=ident_np, sel4=sel_np,
                ones16r=np.ones((1, 128), np.float16),
                ones32r=np.ones((1, 128), np.float32),
                invD16c=np.full((128, 1), 1.0 / D, np.float16),
                esel_in=esel_np,
                cjmb_in=tile(pos256 % 4 - BIG), cj_in=tile(pos256 % 4),
                cgmb_in=tile(pos64 % 4 - BIG), cg_in=tile(pos64 % 4),
                cemb_in=tile(pos256 % 16 - BIG), ce_in=tile(pos256 % 16))


def kernel(L0_attn_x, L0_attn_Wi, L0_attn_bi, L0_attn_Wh, L0_attn_bh,
           L0_ffn_gate_weight, L0_ffn_e_bias, L0_ffn_gate_proj, L0_ffn_up_proj,
           L0_ffn_down_proj):
    x = np.asarray(L0_attn_x, np.float32)
    shared = dict(
        wiT=np.ascontiguousarray(np.asarray(L0_attn_Wi, np.float32).T),
        whT=np.ascontiguousarray(np.asarray(L0_attn_Wh, np.float32).T),
        b_in=np.ascontiguousarray(np.stack([np.asarray(L0_attn_bi, np.float32),
                                            np.asarray(L0_attn_bh, np.float32)])),
        gwT=np.ascontiguousarray(np.asarray(L0_ffn_gate_weight, np.float32).T),
        ebias=np.ascontiguousarray(np.asarray(L0_ffn_e_bias, np.float32)[None, :]),
        gpT=np.ascontiguousarray(np.asarray(L0_ffn_gate_proj, np.float32).transpose(0, 2, 1)),
        upT=np.ascontiguousarray(np.asarray(L0_ffn_up_proj, np.float32).transpose(0, 2, 1)),
        dpT=np.ascontiguousarray(np.asarray(L0_ffn_down_proj, np.float32).transpose(0, 2, 1)),
        **_host_consts(),
    )
    try:
        from concourse.compiler_utils import get_compiler_flags, set_compiler_flags
        set_compiler_flags([f.replace("--enable-ldw-opt=false", "--enable-ldw-opt=true")
                            for f in get_compiler_flags()])
    except Exception:
        pass
    nc = build_nc()
    in_maps = [dict(x_fm=np.ascontiguousarray(x[c].T), **shared) for c in range(8)]
    res = run_bass_kernel_spmd(nc, in_maps, core_ids=list(range(8)))
    out = np.stack([res.results[c]["outT"] for c in range(8)])
    return np.ascontiguousarray(out.transpose(0, 2, 1))


# version 07 (final)
# speedup vs baseline: 1.2063x; 1.0562x over previous
import numpy as np
from contextlib import ExitStack

import concourse.bass as bass
import concourse.mybir as mybir
from concourse.bass_utils import run_bass_kernel_spmd

F32 = mybir.dt.float32
F16 = mybir.dt.float16
AF = mybir.ActivationFunctionType
OP = mybir.AluOpType
AX = mybir.AxisListType

D = 1024
T = 2048
NE = 16
II = 512
KC = D // 128
NTT = T // 128
BIG = 65536.0

COLOF = lambda kc: 4 * (kc % 2) + kc // 2


def build_nc(t_steps=T):
    nc = bass.Bass("TRN2", target_bir_lowering=False)

    x_fm = nc.dram_tensor("x_fm", [D, T], F32, kind="ExternalInput")
    wiT = nc.dram_tensor("wiT", [D, D], F32, kind="ExternalInput")
    whT = nc.dram_tensor("whT", [D, D], F32, kind="ExternalInput")
    b_in = nc.dram_tensor("b_in", [2, D], F32, kind="ExternalInput")
    gwT = nc.dram_tensor("gwT", [D, NE], F32, kind="ExternalInput")
    ebias = nc.dram_tensor("ebias", [1, NE], F32, kind="ExternalInput")
    gpT = nc.dram_tensor("gpT", [NE, D, II], F32, kind="ExternalInput")
    upT = nc.dram_tensor("upT", [NE, D, II], F32, kind="ExternalInput")
    dpT = nc.dram_tensor("dpT", [NE, II, D], F32, kind="ExternalInput")
    ident = nc.dram_tensor("ident", [128, 128], F16, kind="ExternalInput")
    sel4 = nc.dram_tensor("sel4", [128, 4], F16, kind="ExternalInput")
    ones16r = nc.dram_tensor("ones16r", [1, 128], F16, kind="ExternalInput")
    ones32r = nc.dram_tensor("ones32r", [1, 128], F32, kind="ExternalInput")
    invD16c = nc.dram_tensor("invD16c", [128, 1], F16, kind="ExternalInput")
    esel_in = nc.dram_tensor("esel_in", [16, NE * 128], F16, kind="ExternalInput")
    cjmb_in = nc.dram_tensor("cjmb_in", [128, 256], F32, kind="ExternalInput")
    cj_in = nc.dram_tensor("cj_in", [128, 256], F32, kind="ExternalInput")
    cgmb_in = nc.dram_tensor("cgmb_in", [128, 64], F32, kind="ExternalInput")
    cg_in = nc.dram_tensor("cg_in", [128, 64], F32, kind="ExternalInput")
    cemb_in = nc.dram_tensor("cemb_in", [128, 256], F32, kind="ExternalInput")
    ce_in = nc.dram_tensor("ce_in", [128, 256], F32, kind="ExternalInput")
    outT = nc.dram_tensor("outT", [D, T], F32, kind="ExternalOutput")

    es = ExitStack()
    with es:
        id_sb = es.enter_context(nc.sbuf_tensor("id_sb", [128, 128], F16))
        sel_sb = es.enter_context(nc.sbuf_tensor("sel_sb", [128, 4], F16))
        o16r_sb = es.enter_context(nc.sbuf_tensor("o16r_sb", [1, 128], F16))
        o32r_sb = es.enter_context(nc.sbuf_tensor("o32r_sb", [1, 128], F32))
        invD_sb = es.enter_context(nc.sbuf_tensor("invD_sb", [128, 1], F16))
        eps_sb = es.enter_context(nc.sbuf_tensor("eps_sb", [128, 1], F32))
        eb_sb = es.enter_context(nc.sbuf_tensor("eb_sb", [1, NE], F32))
        gwT16 = es.enter_context(nc.sbuf_tensor("gwT16", [128, KC * NE], F16))
        esel_sb = es.enter_context(nc.sbuf_tensor("esel_sb", [16, NE * 128], F16))
        cjmb_sb = es.enter_context(nc.sbuf_tensor("cjmb_sb", [128, 256], F32))
        cj_sb = es.enter_context(nc.sbuf_tensor("cj_sb", [128, 256], F32))
        cgmb_sb = es.enter_context(nc.sbuf_tensor("cgmb_sb", [128, 64], F32))
        cg_sb = es.enter_context(nc.sbuf_tensor("cg_sb", [128, 64], F32))
        cemb_sb = es.enter_context(nc.sbuf_tensor("cemb_sb", [128, 256], F32))
        ce_sb = es.enter_context(nc.sbuf_tensor("ce_sb", [128, 256], F32))

        def sem(name):
            return es.enter_context(nc.semaphore(name))

        s_ld1, s_lg1, s_pe1, s_dv1 = sem("ld1"), sem("lg1"), sem("pe1"), sem("dv1")
        s_mm, s_th, s_tr, s_ev = sem("mm"), sem("th"), sem("tr"), sem("ev")
        s_mm1, s_th1, s_tr1, s_ev1 = sem("mm1"), sem("th1"), sem("tr1"), sem("ev1")
        s_ld3, s_dv3, s_pe3, s_ac3 = sem("ld3"), sem("dv3"), sem("pe3"), sem("ac3")
        s_ld3b = sem("ld3b")
        s_plg, s_asg, s_dvr, s_ptr = sem("plg"), sem("asg"), sem("dvr"), sem("ptr")
        s_lg4, s_pwb, s_dwb, s_pg, s_asl = sem("lg4"), sem("pwb"), sem("dwb"), sem("pg"), sem("asl")
        s_dvm, s_pd, s_dev = sem("dvm"), sem("pd"), sem("dev")
        s_dv5, s_pe5, s_ac5, s_st = sem("dv5"), sem("pe5"), sem("ac5"), sem("st")

        hold_a = ExitStack()
        xi_stream = hold_a.enter_context(nc.sbuf_tensor("xi_stream", [128, NTT * D], F16))
        whT16 = hold_a.enter_context(nc.sbuf_tensor("whT16", [128, KC * D], F16))

        with ExitStack() as p1:
            x16 = p1.enter_context(nc.sbuf_tensor("x16", [128, KC * T], F16))
            wiT16 = p1.enter_context(nc.sbuf_tensor("wiT16", [128, KC * D], F16))
            bi_sb = p1.enter_context(nc.sbuf_tensor("bi_sb", [1, D], F32))
            bh_sb = p1.enter_context(nc.sbuf_tensor("bh_sb", [1, D], F32))
            brow = p1.enter_context(nc.sbuf_tensor("brow", [1, D], F32))
            brow16 = p1.enter_context(nc.sbuf_tensor("brow16", [1, D], F16))
            bias_sb = p1.enter_context(nc.sbuf_tensor("bias_sb", [128, D], F32))
            ps = [p1.enter_context(nc.psum_tensor(f"p1ps{i}", [128, 512], F32)) for i in range(2)]

            with nc.Block() as blk:
                @blk.sync
                def _(sync):
                    loads = [(id_sb, ident), (sel_sb, sel4), (o16r_sb, ones16r),
                             (o32r_sb, ones32r), (invD_sb, invD16c), (bi_sb, b_in[0:1, :]), (bh_sb, b_in[1:2, :]),
                             (eb_sb, ebias), (esel_sb, esel_in), (cjmb_sb, cjmb_in),
                             (cj_sb, cj_in), (cgmb_sb, cgmb_in), (cg_sb, cg_in),
                             (cemb_sb, cemb_in), (ce_sb, ce_in)]
                    for dst, srcap in loads:
                        if not isinstance(srcap, bass.AP):
                            srcap = srcap[:]
                        sync.dma_start(out=dst[:], in_=srcap).then_inc(s_ld1, 16)

                @blk.gpsimd
                def _(gp):
                    gp.dma_start(out=x16[:].rearrange("p (c t) -> p c t", c=KC),
                                 in_=x_fm[:].rearrange("(c p) t -> p c t", p=128)).then_inc(s_lg1, 16)
                    gp.dma_start(out=wiT16[:].rearrange("p (c n) -> p c n", c=KC),
                                 in_=wiT[:].rearrange("(c p) n -> p c n", p=128)).then_inc(s_lg1, 16)
                    gp.dma_start(out=whT16[:].rearrange("p (c n) -> p c n", c=KC),
                                 in_=whT[:].rearrange("(c p) n -> p c n", p=128)).then_inc(s_lg1, 16)
                    gp.dma_start(out=gwT16[:].rearrange("p (c n) -> p c n", c=KC),
                                 in_=gwT[:].rearrange("(c p) n -> p c n", p=128)).then_inc(s_lg1, 16)

                @blk.vector
                def _(v):
                    v.memset(eps_sb[:], 1e-5)
                    v.wait_ge(s_ld1, 16 * 15)
                    v.tensor_tensor(out=brow[:], in0=bi_sb[0:1, :], in1=bh_sb[0:1, :], op=OP.add).then_inc(s_dv1, 1)
                    v.wait_ge(s_dv1, 1)
                    v.tensor_copy(out=brow16[:], in_=brow[:]).then_inc(s_dv1, 1)
                    for h in range(2):
                        v.wait_ge(s_pe1, 1 + h)
                        v.tensor_copy(out=bias_sb[:, 512 * h:512 * (h + 1)], in_=ps[h][:, :]).then_inc(s_dv1, 1)
                    for n in range(2 * NTT):
                        v.wait_ge(s_pe1, 3 + n)
                        m, h = n // 2, n % 2
                        v.tensor_tensor(out=xi_stream[:, m * D + 512 * h: m * D + 512 * (h + 1)],
                                        in0=ps[n % 2][:, :], in1=bias_sb[:, 512 * h:512 * (h + 1)],
                                        op=OP.add).then_inc(s_dv1, 1)

                @blk.tensor
                def _(te):
                    te.wait_ge(s_dv1, 2)
                    for h in range(2):
                        if h == 1:
                            te.wait_ge(s_pe1, 1)
                        te.matmul(out=ps[h][:, :], lhsT=o16r_sb[:, :],
                                  rhs=brow16[0:1, 512 * h:512 * (h + 1)],
                                  start=True, stop=True).then_inc(s_pe1, 1)
                    te.wait_ge(s_lg1, 64)
                    for n in range(2 * NTT):
                        m, h = n // 2, n % 2
                        te.wait_ge(s_dv1, n + 3)
                        for kc in range(KC):
                            mi = te.matmul(out=ps[n % 2][:, :],
                                           lhsT=x16[:, kc * T + 128 * m: kc * T + 128 * (m + 1)],
                                           rhs=wiT16[:, kc * D + 512 * h: kc * D + 512 * (h + 1)],
                                           start=(kc == 0), stop=(kc == KC - 1))
                        mi.then_inc(s_pe1, 1)

        hold_h = ExitStack()
        buf_h = hold_h.enter_context(nc.sbuf_tensor("buf_h", [128, KC * T], F32, side="right"))
        HFREE = KC * T

        with ExitStack() as p2:
            hsp = p2.enter_context(nc.sbuf_tensor("hsp", [128, 512], F16))
            hstat = p2.enter_context(nc.sbuf_tensor("hstat", [128, 16], F16))
            zA = [p2.enter_context(nc.psum_tensor(f"zA{i}", [128, 512], F32)) for i in range(2)]
            zB = [p2.enter_context(nc.psum_tensor(f"zB{i}", [128, 512], F32)) for i in range(2)]
            tpA = [p2.enter_context(nc.psum_tensor(f"tpA{i}", [128, 4], F32)) for i in range(2)]
            tpB = [p2.enter_context(nc.psum_tensor(f"tpB{i}", [128, 4], F32)) for i in range(2)]

            with nc.Block() as blk:
                @blk.vector
                def _(v):
                    if t_steps < T:
                        for cc in range(KC):
                            v.memset(buf_h[:, cc * T:(cc + 1) * T], 0.0)
                    for b in zA + zB:
                        v.memset(b[:], 0.0)
                    v.memset(hstat[:], 0.0).then_inc(s_ev, 1)
                    v.wait_ge(s_ev, 1)
                    v.memset(hsp[:, 0:1], 0.0).then_inc(s_ev1, 1)
                    for t in range(t_steps):
                        par = t % 2
                        hs = (1 - par) * 8
                        v.wait_ge(s_tr, t + 1)
                        v.tensor_copy(out=hstat[:, hs:hs + 4], in_=tpA[par][:, :])
                        v.tensor_copy(out=bass.AP(buf_h, t, [[HFREE, 128], [2 * T, 4]]),
                                      in_=bass.AP(tpA[par], 0, [[4, 128], [1, 4]])).then_inc(s_ev, 1)
                        v.wait_ge(s_tr1, t + 1)
                        v.tensor_copy(out=hstat[:, hs + 4:hs + 8], in_=tpB[par][:, :])
                        v.tensor_copy(out=bass.AP(buf_h, t + T, [[HFREE, 128], [2 * T, 4]]),
                                      in_=bass.AP(tpB[par], 0, [[4, 128], [1, 4]])).then_inc(s_ev1, 1)

                @blk.tensor
                def _(te):
                    halves = [(zA, 0), (zB, 128)]
                    for t in range(t_steps):
                        par = t % 2
                        blkofs = (t // 128) * D
                        r = t % 128
                        if t >= 2:
                            te.wait_ge(s_th1, t - 1)
                        te.wait_ge(s_ev, t + 1)
                        for zb_, ofs in halves:
                            for g in range(4):
                                te.matmul(out=zb_[par][32 * g:32 * g + 32, 0:128],
                                          lhsT=bass.AP(id_sb, r, [[128, 128], [0, 32]]),
                                          rhs=xi_stream[:, blkofs + 256 * g + ofs: blkofs + 256 * g + ofs + 128],
                                          start=True, stop=False, tile_position=(0, 32 * g),
                                          skip_group_check=True)
                        for zb_, ofs in halves:
                            for kc in range(0, KC, 2):
                                for g in range(4):
                                    te.matmul(out=zb_[par][32 * g:32 * g + 32, 0:128],
                                              lhsT=bass.AP(hstat, par * 8 + COLOF(kc), [[16, 128], [0, 32]]),
                                              rhs=whT16[:, kc * D + 256 * g + ofs: kc * D + 256 * g + ofs + 128],
                                              start=False, stop=False, tile_position=(0, 32 * g),
                                              skip_group_check=True)
                        te.wait_ge(s_ev1, t + 1)
                        for zb_, ofs, s_done in [(zA, 0, s_mm), (zB, 128, s_mm1)]:
                            for kc in range(1, KC, 2):
                                for g in range(4):
                                    mi = te.matmul(out=zb_[par][32 * g:32 * g + 32, 0:128],
                                                   lhsT=bass.AP(hstat, par * 8 + COLOF(kc), [[16, 128], [0, 32]]),
                                                   rhs=whT16[:, kc * D + 256 * g + ofs: kc * D + 256 * g + ofs + 128],
                                                   start=False, stop=(kc == KC - 1), tile_position=(0, 32 * g),
                                                   skip_group_check=True)
                            mi.then_inc(s_done, 1)
                        te.wait_ge(s_th, t + 1)
                        for j in range(4):
                            ti = te.matmul(out=tpA[par][32 * j:32 * j + 32, 0:4],
                                           lhsT=hsp[:, par * 256 + 32 * j: par * 256 + 32 * j + 32],
                                           rhs=sel_sb[:, :], start=True, stop=True,
                                           tile_position=(0, 32 * j))
                        ti.then_inc(s_tr, 1)
                        te.wait_ge(s_th1, t + 1)
                        for j in range(4):
                            ti = te.matmul(out=tpB[par][32 * j:32 * j + 32, 0:4],
                                           lhsT=hsp[:, par * 256 + 128 + 32 * j: par * 256 + 128 + 32 * j + 32],
                                           rhs=sel_sb[:, :], start=True, stop=True,
                                           tile_position=(0, 32 * j))
                        ti.then_inc(s_tr1, 1)

                @blk.scalar
                def _(sc):
                    for t in range(t_steps):
                        par = t % 2
                        sc.wait_ge(s_mm, t + 1)
                        sc.activation(out=hsp[:, par * 256:par * 256 + 128],
                                      in_=zA[par][:, 0:128], func=AF.Tanh).then_inc(s_th, 1)
                        sc.wait_ge(s_mm1, t + 1)
                        sc.activation(out=hsp[:, par * 256 + 128:par * 256 + 256],
                                      in_=zB[par][:, 0:128], func=AF.Tanh).then_inc(s_th1, 1)
        hold_a.close()

        def emit_ln(src, scr, scrv1, r16s, r2s, st_ps, mb_ps,
                    S_DV, S_PE, S_AC, pre_chunk_dve, n_pre,
                    extra_dve_tail, n_tail, sync_body=None):
            m_row = lambda n: scr[0:1, 1 * T + 512 * n: 1 * T + 512 * (n + 1)]
            s_row = lambda n: scr[0:1, 2 * T + 512 * n: 2 * T + 512 * (n + 1)]
            q_row = lambda n: scr[0:1, 3 * T + 512 * n: 3 * T + 512 * (n + 1)]
            m_rowF = scr[0:1, 1 * T:1 * T + T]
            s_rowF = scr[0:1, 2 * T:2 * T + T]
            q_rowF = scr[0:1, 3 * T:3 * T + T]

            per_c = n_pre + 2
            rows0 = KC * per_c
            rows_end = rows0 + 11

            with nc.Block() as blk:
                if sync_body is not None:
                    @blk.sync
                    def _(sync):
                        sync_body(sync)

                @blk.vector
                def _(v):
                    dvc = 0
                    for cc in range(KC):
                        if pre_chunk_dve:
                            dvc = pre_chunk_dve(v, cc, dvc)
                        if cc >= 1:
                            v.wait_ge(S_PE, cc)
                        v.wait_ge(S_DV, dvc)
                        v.tensor_copy(out=r16s[:, :], in_=src[:, cc * T:(cc + 1) * T]).then_inc(S_DV, 1)
                        dvc += 1
                        v.wait_ge(S_DV, dvc)
                        v.tensor_tensor(out=r2s[:, :], in0=src[:, cc * T:(cc + 1) * T],
                                        in1=src[:, cc * T:(cc + 1) * T], op=OP.mult).then_inc(S_DV, 1)
                        dvc += 1
                    assert dvc == rows0
                    v.wait_ge(S_PE, KC)
                    for n in range(4):
                        v.wait_ge(S_DV, dvc)
                        v.tensor_copy(out=m_row(n), in_=st_ps[n][0:1, :]).then_inc(S_DV, 1)
                        dvc += 1
                    v.wait_ge(S_DV, dvc)
                    v.tensor_tensor(out=q_rowF, in0=m_rowF, in1=m_rowF, op=OP.mult).then_inc(S_DV, 1)
                    dvc += 1
                    for n in range(4):
                        v.wait_ge(S_DV, dvc)
                        v.tensor_tensor(out=s_row(n), in0=st_ps[n][32:33, :], in1=q_row(n),
                                        op=OP.subtract).then_inc(S_DV, 1)
                        dvc += 1
                    v.wait_ge(S_AC, 4)
                    v.reciprocal(out=s_rowF, in_=s_rowF).then_inc(S_DV, 1)
                    dvc += 1
                    v.wait_ge(S_DV, dvc)
                    v.tensor_tensor(out=q_rowF, in0=m_rowF, in1=s_rowF, op=OP.mult).then_inc(S_DV, 1)
                    dvc += 1
                    assert dvc == rows_end
                    v.wait_ge(S_PE, KC + 8)
                    for cc in range(KC):
                        for n in range(4):
                            sl = slice(cc * T + 512 * n, cc * T + 512 * (n + 1))
                            v.wait_ge(S_DV, dvc)
                            v.tensor_tensor(out=scrv1[:, :], in0=src[:, sl], in1=mb_ps[n][:, :],
                                            op=OP.mult).then_inc(S_DV, 1)
                            dvc += 1
                            v.wait_ge(S_DV, dvc)
                            dvc = extra_dve_tail(v, cc, n, dvc, scrv1)

                @blk.tensor
                def _(te):
                    for cc in range(KC):
                        te.wait_ge(S_DV, n_pre * (cc + 1) + 2 * (cc + 1))
                        for n in range(4):
                            te.matmul(out=st_ps[n][0:1, :], lhsT=invD_sb[:, :],
                                      rhs=r16s[:, 512 * n:512 * (n + 1)],
                                      start=(cc == 0), stop=(cc == KC - 1), tile_position=(0, 0))
                            mi = te.matmul(out=st_ps[n][32:33, :], lhsT=invD_sb[:, :],
                                           rhs=r2s[:, 512 * n:512 * (n + 1)],
                                           start=(cc == 0), stop=(cc == KC - 1), tile_position=(0, 32))
                        mi.then_inc(S_PE, 1)
                    te.wait_ge(S_DV, rows_end)
                    k = KC
                    for n in range(4):
                        te.wait_ge(S_PE, k)
                        te.matmul(out=mb_ps[n][:, :], lhsT=o32r_sb[:, :], rhs=s_row(n),
                                  start=True, stop=True).then_inc(S_PE, 1)
                        k += 1
                        te.wait_ge(S_PE, k)
                        te.matmul(out=st_ps[n][:, :], lhsT=o32r_sb[:, :], rhs=q_row(n),
                                  start=True, stop=True).then_inc(S_PE, 1)
                        k += 1

                @blk.scalar
                def _(sc):
                    for n in range(4):
                        sc.wait_ge(S_DV, rows0 + 6 + n)
                        if n > 0:
                            sc.wait_ge(S_AC, n)
                        sc.activation(out=s_row(n), in_=s_row(n), func=AF.Sqrt,
                                      scale=1.0, bias=eps_sb[0:1, 0:1]).then_inc(S_AC, 1)

        buf_u = es.enter_context(nc.sbuf_tensor("buf_u", [128, KC * T], F32))
        hold_u16 = ExitStack()
        u16 = hold_u16.enter_context(nc.sbuf_tensor("u16", [128, KC * T], F16))
        combT16 = hold_u16.enter_context(nc.sbuf_tensor("combT16", [16, T], F16))

        with ExitStack() as p3:
            xbuf = p3.enter_context(nc.sbuf_tensor("xbuf", [128, 2 * T], F32))
            r16s = p3.enter_context(nc.sbuf_tensor("r16s3", [128, T], F16))
            r2s = p3.enter_context(nc.sbuf_tensor("r2s3", [128, T], F16))
            st_ps = [p3.enter_context(nc.psum_tensor(f"st3{i}", [128, 512], F32)) for i in range(4)]
            mb_ps = [p3.enter_context(nc.psum_tensor(f"mb3{i}", [128, 512], F32)) for i in range(4)]
            scrv1 = p3.enter_context(nc.sbuf_tensor("scrv13", [128, 512], F32))

            def pre3(v, cc, dvc):
                if cc % 2 == 0:
                    v.wait_ge(s_ld3, 16 * (cc // 2 + 1))
                else:
                    v.wait_ge(s_ld3b, 16 * (cc // 2 + 1))
                v.wait_ge(s_dv3, dvc)
                sl_x = slice((cc % 2) * T, (cc % 2) * T + T)
                v.tensor_tensor(out=buf_u[:, cc * T:(cc + 1) * T], in0=xbuf[:, sl_x],
                                in1=buf_h[:, cc * T:(cc + 1) * T], op=OP.add).then_inc(s_dv3, 1)
                return dvc + 1

            def tail3(v, cc, n, dvc, scrv1_):
                sl = slice(cc * T + 512 * n, cc * T + 512 * (n + 1))
                v.tensor_tensor(out=buf_u[:, sl], in0=scrv1_[:, :], in1=st_ps[n][:, :],
                                op=OP.subtract).then_inc(s_dv3, 1)
                dvc += 1
                v.wait_ge(s_dv3, dvc)
                v.tensor_copy(out=u16[:, sl], in_=buf_u[:, sl]).then_inc(s_dv3, 1)
                return dvc + 1

            def sync3(sync):
                for cc in range(KC):
                    if cc >= 2:
                        sync.wait_ge(s_dv3, 3 * (cc - 1) - 2)
                    sync.dma_start(out=xbuf[:, (cc % 2) * T:(cc % 2) * T + T],
                                   in_=x_fm[128 * cc:128 * (cc + 1), :]).then_inc(
                                       s_ld3 if cc % 2 == 0 else s_ld3b, 16)

            emit_ln(src=buf_u, scr=buf_h, scrv1=scrv1, r16s=r16s, r2s=r2s,
                    st_ps=st_ps, mb_ps=mb_ps, S_DV=s_dv3, S_PE=s_pe3, S_AC=s_ac3,
                    pre_chunk_dve=pre3, n_pre=1, extra_dve_tail=tail3, n_tail=2,
                    sync_body=sync3)

        hold_h.close()

        with ExitStack() as p3b:
            scores = p3b.enter_context(nc.sbuf_tensor("scores", [128, 256], F32))
            combine16 = p3b.enter_context(nc.sbuf_tensor("combine16", [128, 256], F16))
            R = [p3b.enter_context(nc.sbuf_tensor(f"rtR{i}", [128, 256], F32)) for i in range(5)]
            G = [p3b.enter_context(nc.sbuf_tensor(f"rtG{i}", [128, 64], F32)) for i in range(4)]
            H = [p3b.enter_context(nc.sbuf_tensor(f"rtH{i}", [128, 16], F32)) for i in range(4)]
            lg_ps = [p3b.enter_context(nc.psum_tensor(f"lg{i}", [128, 512], F32)) for i in range(2)]
            eb_ps = p3b.enter_context(nc.psum_tensor("ebps", [128, 512], F32))
            tr_ps = [p3b.enter_context(nc.psum_tensor(f"trps{i}", [16, 512], F32)) for i in range(2)]
            rt_ops = [0]

            with nc.Block() as blk:
                @blk.vector
                def _(v):
                    dvc = [0]

                    def op(fn):
                        v.wait_ge(s_dvr, dvc[0])
                        fn().then_inc(s_dvr, 1)
                        dvc[0] += 1

                    def bc(t, inner, rep):
                        f = t.shape[1]
                        return bass.AP(t, 0, [[f, 128], [1, f], [0, rep]]) if inner == f else None

                    v.wait_ge(s_asg, NTT)
                    v.wait_ge(s_plg, 1)
                    A4 = lambda t: bass.AP(t, 0, [[256, 128], [4, 64], [1, 4]])
                    A16 = lambda t: bass.AP(t, 0, [[256, 128], [16, 16], [1, 16]])
                    B4_64 = lambda t: bass.AP(t, 0, [[64, 128], [1, 64], [0, 4]])
                    B4_16 = lambda t: bass.AP(t, 0, [[16, 128], [1, 16], [0, 4]])
                    B16 = lambda t: bass.AP(t, 0, [[16, 128], [1, 16], [0, 16]])
                    G4 = lambda t: bass.AP(t, 0, [[64, 128], [4, 16], [1, 4]])

                    op(lambda: v.tensor_tensor(out=R[0][:, :], in0=scores[:, :],
                                               in1=bass.AP(eb_ps, 0, [[512, 128], [0, 16], [1, 16]]),
                                               op=OP.add))
                    op(lambda: v.tensor_reduce(out=G[0][:, :], in_=A4(R[0]), axis=AX.X, op=OP.max))
                    op(lambda: v.tensor_tensor(out=R[1][:, :], in0=R[0][:, :], in1=B4_64(G[0]), op=OP.is_equal))
                    op(lambda: v.tensor_tensor(out=R[2][:, :], in0=R[1][:, :], in1=cjmb_sb[:, :], op=OP.mult))
                    op(lambda: v.tensor_scalar(out=R[2][:, :], in0=R[2][:, :], scalar1=BIG,
                                               scalar2=None, op0=OP.add))
                    op(lambda: v.tensor_reduce(out=G[1][:, :], in_=A4(R[2]), axis=AX.X, op=OP.min))
                    op(lambda: v.tensor_tensor(out=R[1][:, :], in0=cj_sb[:, :], in1=B4_64(G[1]), op=OP.is_equal))
                    op(lambda: v.scalar_tensor_tensor(out=R[2][:, :], in0=R[1][:, :], scalar=-BIG,
                                                      in1=R[0][:, :], op0=OP.mult, op1=OP.add))
                    op(lambda: v.tensor_reduce(out=G[1][:, :], in_=A4(R[2]), axis=AX.X, op=OP.max))
                    op(lambda: v.tensor_tensor(out=G[0][:, :], in0=G[0][:, :], in1=G[1][:, :], op=OP.add))
                    op(lambda: v.tensor_reduce(out=H[0][:, :], in_=G4(G[0]), axis=AX.X, op=OP.max))
                    op(lambda: v.tensor_tensor(out=G[1][:, :], in0=G[0][:, :], in1=B4_16(H[0]), op=OP.is_equal))
                    op(lambda: v.tensor_tensor(out=G[2][:, :], in0=G[1][:, :], in1=cgmb_sb[:, :], op=OP.mult))
                    op(lambda: v.tensor_scalar(out=G[2][:, :], in0=G[2][:, :], scalar1=BIG,
                                               scalar2=None, op0=OP.add))
                    op(lambda: v.tensor_reduce(out=H[0][:, :], in_=G4(G[2]), axis=AX.X, op=OP.min))
                    op(lambda: v.tensor_tensor(out=G[1][:, :], in0=cg_sb[:, :], in1=B4_16(H[0]), op=OP.is_equal))
                    op(lambda: v.scalar_tensor_tensor(out=G[2][:, :], in0=G[1][:, :], scalar=-BIG,
                                                      in1=G[0][:, :], op0=OP.mult, op1=OP.add))
                    op(lambda: v.tensor_reduce(out=H[0][:, :], in_=G4(G[2]), axis=AX.X, op=OP.max))
                    op(lambda: v.tensor_tensor(out=G[3][:, :], in0=G[2][:, :], in1=B4_16(H[0]), op=OP.is_equal))
                    op(lambda: v.tensor_tensor(out=G[3][:, :], in0=G[3][:, :], in1=cgmb_sb[:, :], op=OP.mult))
                    op(lambda: v.tensor_scalar(out=G[3][:, :], in0=G[3][:, :], scalar1=BIG,
                                               scalar2=None, op0=OP.add))
                    op(lambda: v.tensor_reduce(out=H[0][:, :], in_=G4(G[3]), axis=AX.X, op=OP.min))
                    op(lambda: v.tensor_tensor(out=G[2][:, :], in0=cg_sb[:, :], in1=B4_16(H[0]), op=OP.is_equal))
                    op(lambda: v.tensor_tensor(out=G[1][:, :], in0=G[1][:, :], in1=G[2][:, :], op=OP.add))
                    op(lambda: v.tensor_tensor(out=R[2][:, :], in0=R[0][:, :], in1=B4_64(G[1]), op=OP.mult))
                    for k in range(4):
                        op(lambda: v.tensor_reduce(out=H[0][:, :], in_=A16(R[2]), axis=AX.X, op=OP.max))
                        op(lambda: v.tensor_tensor(out=R[1][:, :], in0=R[2][:, :], in1=B16(H[0]), op=OP.is_equal))
                        op(lambda: v.tensor_tensor(out=R[3][:, :], in0=R[1][:, :], in1=cemb_sb[:, :], op=OP.mult))
                        op(lambda: v.tensor_scalar(out=R[3][:, :], in0=R[3][:, :], scalar1=BIG,
                                                   scalar2=None, op0=OP.add))
                        op(lambda: v.tensor_reduce(out=H[0][:, :], in_=A16(R[3]), axis=AX.X, op=OP.min))
                        op(lambda: v.tensor_tensor(out=R[1][:, :], in0=ce_sb[:, :], in1=B16(H[0]), op=OP.is_equal))
                        op(lambda: v.tensor_tensor(out=R[3][:, :], in0=R[1][:, :], in1=scores[:, :], op=OP.mult))
                        op(lambda: v.tensor_reduce(out=H[1][:, :], in_=A16(R[3]), axis=AX.X, op=OP.add))
                        if k == 0:
                            op(lambda: v.tensor_copy(out=H[2][:, :], in_=H[1][:, :]))
                            op(lambda: v.tensor_tensor(out=R[4][:, :], in0=R[1][:, :], in1=B16(H[1]), op=OP.mult))
                        else:
                            op(lambda: v.tensor_tensor(out=H[2][:, :], in0=H[2][:, :], in1=H[1][:, :], op=OP.add))
                            op(lambda: v.tensor_tensor(out=R[3][:, :], in0=R[1][:, :], in1=B16(H[1]), op=OP.mult))
                            op(lambda: v.tensor_tensor(out=R[4][:, :], in0=R[4][:, :], in1=R[3][:, :], op=OP.add))
                        op(lambda: v.scalar_tensor_tensor(out=R[2][:, :], in0=R[1][:, :], scalar=-BIG,
                                                          in1=R[2][:, :], op0=OP.mult, op1=OP.add))
                    op(lambda: v.tensor_scalar(out=H[2][:, :], in0=H[2][:, :], scalar1=1e-20,
                                               scalar2=None, op0=OP.add))
                    op(lambda: v.reciprocal(out=H[3][:, :], in_=H[2][:, :]))
                    op(lambda: v.tensor_tensor(out=R[4][:, :], in0=R[4][:, :], in1=B16(H[3]), op=OP.mult))
                    op(lambda: v.tensor_copy(out=combine16[:, :], in_=R[4][:, :]))
                    rt_ops[0] = dvc[0]
                    for rnd in range(4):
                        v.wait_ge(s_ptr, 4 * (rnd + 1))
                        v.wait_ge(s_dvr, dvc[0])
                        v.tensor_copy(out=combT16[:, 512 * rnd:512 * (rnd + 1)],
                                      in_=tr_ps[rnd % 2][:, :]).then_inc(s_dvr, 1)
                        dvc[0] += 1

                @blk.tensor
                def _(te):
                    te.matmul(out=eb_ps[:, 0:NE], lhsT=o32r_sb[:, :], rhs=eb_sb[0:1, :],
                              start=True, stop=True).then_inc(s_plg, 1)
                    for m in range(NTT):
                        if m >= 2:
                            te.wait_ge(s_asg, m - 1)
                        te.wait_ge(s_plg, m + 1)
                        for cc in range(KC):
                            mi = te.matmul(out=lg_ps[m % 2][:, 0:NE],
                                           lhsT=u16[:, cc * T + 128 * m: cc * T + 128 * (m + 1)],
                                           rhs=gwT16[:, cc * NE:(cc + 1) * NE],
                                           start=(cc == 0), stop=(cc == KC - 1))
                        mi.then_inc(s_plg, 1)
                    RT = rt_ops[0]
                    assert RT > 0
                    te.wait_ge(s_dvr, RT)
                    for q in range(NTT):
                        b, ofs = (q // 4) % 2, (q % 4) * 128
                        if q // 4 >= 2:
                            te.wait_ge(s_dvr, RT + (q // 4 - 1))
                        te.wait_ge(s_ptr, q)
                        te.matmul(out=tr_ps[b][:, ofs:ofs + 128],
                                  lhsT=combine16[:, 16 * q:16 * (q + 1)], rhs=id_sb[:, :],
                                  start=True, stop=True).then_inc(s_ptr, 1)

                @blk.scalar
                def _(sc):
                    for m in range(NTT):
                        sc.wait_ge(s_plg, m + 2)
                        if m > 0:
                            sc.wait_ge(s_asg, m)
                        sc.activation(out=scores[:, 16 * m:16 * (m + 1)], in_=lg_ps[m % 2][:, 0:NE],
                                      func=AF.Sigmoid).then_inc(s_asg, 1)

        with ExitStack() as p4:
            wg16 = [p4.enter_context(nc.sbuf_tensor(f"wg16_{i}", [128, KC * II], F16)) for i in range(2)]
            wu16 = [p4.enter_context(nc.sbuf_tensor(f"wu16_{i}", [128, KC * II], F16)) for i in range(2)]
            wd16 = [p4.enter_context(nc.sbuf_tensor(f"wd16_{i}", [128, 4 * D], F16)) for i in range(2)]
            wbc_sb = p4.enter_context(nc.sbuf_tensor("wbc_sb", [128, T], F32))
            act16 = p4.enter_context(nc.sbuf_tensor("act16", [128, 4 * 512], F16))
            s1b = [p4.enter_context(nc.sbuf_tensor(f"s1b{i}", [128, 512], F32)) for i in range(2)]
            s2b = [p4.enter_context(nc.sbuf_tensor(f"s2b{i}", [128, 512], F32)) for i in range(2)]
            ga_ps = [p4.enter_context(nc.psum_tensor(f"ga{i}", [128, 512], F32)) for i in range(2)]
            up_ps = [p4.enter_context(nc.psum_tensor(f"up{i}", [128, 512], F32)) for i in range(2)]
            dn_ps = [p4.enter_context(nc.psum_tensor(f"dn{i}", [128, 512], F32)) for i in range(2)]
            wb_ps = [p4.enter_context(nc.psum_tensor(f"wb{i}", [128, 512], F32)) for i in range(2)]

            with nc.Block() as blk:
                @blk.gpsimd
                def _(gp):
                    for e in range(NE):
                        if e >= 2:
                            gp.wait_ge(s_pd, 32 * (e - 1))
                        gp.wait_ge(s_lg4, 48 * e)
                        sl = e % 2
                        gp.dma_start(out=wg16[sl][:].rearrange("p (c i) -> p c i", c=KC),
                                     in_=gpT[e].rearrange("(c p) i -> p c i", p=128)).then_inc(s_lg4, 16)
                        gp.dma_start(out=wu16[sl][:].rearrange("p (c i) -> p c i", c=KC),
                                     in_=upT[e].rearrange("(c p) i -> p c i", p=128)).then_inc(s_lg4, 16)
                        gp.dma_start(out=wd16[sl][:].rearrange("p (c h) -> p c h", c=4),
                                     in_=dpT[e].rearrange("(c p) h -> p c h", p=128)).then_inc(s_lg4, 16)

                @blk.tensor
                def _(te):
                    for e in range(NE):
                        sl = e % 2
                        te.wait_ge(s_lg4, 48 * (e + 1))
                        te.wait_ge(s_dvm, 32 * e)
                        for tt in range(4):
                            te.wait_ge(s_dwb, 4 * e + max(0, tt - 1))
                            te.wait_ge(s_pwb, 4 * e + tt)
                            te.matmul(out=wb_ps[tt % 2][:, :],
                                      lhsT=esel_sb[:, 128 * e:128 * (e + 1)],
                                      rhs=combT16[:, 512 * tt:512 * (tt + 1)],
                                      start=True, stop=True).then_inc(s_pwb, 1)
                        for tt in range(4):
                            for ic in range(4):
                                q = 16 * e + 4 * tt + ic
                                if q >= 2:
                                    te.wait_ge(s_asl, q - 1)
                                    te.wait_ge(s_dvm, 2 * q - 3)
                                for kc in range(KC):
                                    mi = te.matmul(out=ga_ps[ic % 2][:, :],
                                                   lhsT=wg16[sl][:, kc * II + 128 * ic: kc * II + 128 * (ic + 1)],
                                                   rhs=u16[:, kc * T + 512 * tt: kc * T + 512 * (tt + 1)],
                                                   start=(kc == 0), stop=(kc == KC - 1))
                                mi.then_inc(s_pg, 1)
                                for kc in range(KC):
                                    mi = te.matmul(out=up_ps[ic % 2][:, :],
                                                   lhsT=wu16[sl][:, kc * II + 128 * ic: kc * II + 128 * (ic + 1)],
                                                   rhs=u16[:, kc * T + 512 * tt: kc * T + 512 * (tt + 1)],
                                                   start=(kc == 0), stop=(kc == KC - 1))
                                mi.then_inc(s_pg, 1)
                            base_q = 16 * e + 4 * tt
                            for hc in range(KC):
                                gidx = 32 * e + 8 * tt + hc
                                if gidx >= 2:
                                    te.wait_ge(s_dev, gidx - 1)
                                for kci in range(4):
                                    if hc == 0:
                                        te.wait_ge(s_dvm, 2 * (base_q + kci + 1))
                                    mi = te.matmul(out=dn_ps[hc % 2][:, :],
                                                   lhsT=wd16[sl][:, kci * D + 128 * hc: kci * D + 128 * (hc + 1)],
                                                   rhs=act16[:, 512 * kci:512 * (kci + 1)],
                                                   start=(kci == 0), stop=(kci == 3))
                                mi.then_inc(s_pd, 1)

                @blk.scalar
                def _(sc):
                    for e in range(NE):
                        for tt in range(4):
                            for ic in range(4):
                                q = 16 * e + 4 * tt + ic
                                sc.wait_ge(s_pg, 2 * q + 1)
                                if q >= 2:
                                    sc.wait_ge(s_dvm, 2 * q - 3)
                                sc.activation(out=s1b[q % 2][:, :], in_=ga_ps[ic % 2][:, :],
                                              func=AF.Silu).then_inc(s_asl, 1)

                @blk.vector
                def _(v):
                    for e in range(NE):
                        for tt in range(4):
                            v.wait_ge(s_pwb, 4 * e + tt + 1)
                            v.wait_ge(s_dwb, 4 * e + tt)
                            v.tensor_copy(out=wbc_sb[:, 512 * tt:512 * (tt + 1)],
                                          in_=wb_ps[tt % 2][:, :]).then_inc(s_dwb, 1)
                        for tt in range(4):
                            for ic in range(4):
                                q = 16 * e + 4 * tt + ic
                                v.wait_ge(s_asl, q + 1)
                                v.wait_ge(s_pg, 2 * q + 2)
                                if q >= 2:
                                    v.wait_ge(s_dvm, 2 * q - 2)
                                if ic == 0:
                                    v.wait_ge(s_dwb, 4 * (e + 1))
                                    if tt >= 1:
                                        v.wait_ge(s_pd, 32 * e + 8 * tt)
                                    elif e >= 1:
                                        v.wait_ge(s_pd, 32 * e)
                                v.tensor_tensor(out=s2b[q % 2][:, :], in0=up_ps[ic % 2][:, :],
                                                in1=s1b[q % 2][:, :], op=OP.mult).then_inc(s_dvm, 1)
                                v.wait_ge(s_dvm, 2 * q + 1)
                                v.tensor_tensor(out=act16[:, 512 * ic:512 * (ic + 1)],
                                                in0=s2b[q % 2][:, :],
                                                in1=wbc_sb[:, 512 * tt:512 * (tt + 1)],
                                                op=OP.mult).then_inc(s_dvm, 1)
                            for hc in range(KC):
                                gidx = 32 * e + 8 * tt + hc
                                v.wait_ge(s_pd, gidx + 1)
                                sl4 = slice(hc * T + 512 * tt, hc * T + 512 * (tt + 1))
                                v.tensor_tensor(out=buf_u[:, sl4], in0=dn_ps[hc % 2][:, :],
                                                in1=buf_u[:, sl4], op=OP.add).then_inc(s_dev, 1)
        hold_u16.close()

        LN5_ROWS_END = KC * 2 + 11
        with ExitStack() as p5:
            scr5 = p5.enter_context(nc.sbuf_tensor("scr5", [128, 4 * T], F32))
            r16s5 = p5.enter_context(nc.sbuf_tensor("r16s5", [128, T], F16))
            r2s5 = p5.enter_context(nc.sbuf_tensor("r2s5", [128, T], F16))
            och = p5.enter_context(nc.sbuf_tensor("och", [128, 2 * T], F32))
            st_ps5 = [p5.enter_context(nc.psum_tensor(f"st5{i}", [128, 512], F32)) for i in range(4)]
            mb_ps5 = [p5.enter_context(nc.psum_tensor(f"mb5{i}", [128, 512], F32)) for i in range(4)]
            scrv15 = p5.enter_context(nc.sbuf_tensor("scrv15", [128, 512], F32))

            def tail5(v, cc, n, dvc, scrv1_):
                if cc >= 2 and n == 0:
                    v.wait_ge(s_st, 16 * (cc - 1))
                sl = slice((cc % 2) * T + 512 * n, (cc % 2) * T + 512 * (n + 1))
                v.tensor_tensor(out=och[:, sl], in0=scrv1_[:, :], in1=st_ps5[n][:, :],
                                op=OP.subtract).then_inc(s_dv5, 1)
                return dvc + 1

            def sync5(sync):
                for cc in range(KC):
                    sync.wait_ge(s_dv5, LN5_ROWS_END + 8 * (cc + 1))
                    sync.wait_ge(s_st, 16 * cc)
                    sync.dma_start(out=outT[128 * cc:128 * (cc + 1), :],
                                   in_=och[:, (cc % 2) * T:(cc % 2 + 1) * T]).then_inc(s_st, 16)
                sync.wait_ge(s_st, 16 * KC)

            emit_ln(src=buf_u, scr=scr5, scrv1=scrv15, r16s=r16s5, r2s=r2s5,
                    st_ps=st_ps5, mb_ps=mb_ps5, S_DV=s_dv5, S_PE=s_pe5, S_AC=s_ac5,
                    pre_chunk_dve=None, n_pre=0, extra_dve_tail=tail5, n_tail=1,
                    sync_body=sync5)

    return nc


def _host_consts():
    ident_np = np.eye(128, dtype=np.float16)
    sel_np = np.zeros((128, 4), np.float16)
    for n in range(4):
        sel_np[32 * n, n] = 1.0
    esel_np = np.zeros((16, NE * 128), np.float16)
    for e in range(NE):
        esel_np[e, 128 * e:128 * (e + 1)] = 1.0
    pos256 = np.arange(256)
    pos64 = np.arange(64)
    tile = lambda v: np.ascontiguousarray(np.tile(v.astype(np.float32), (128, 1)))
    return dict(ident=ident_np, sel4=sel_np,
                ones16r=np.ones((1, 128), np.float16),
                ones32r=np.ones((1, 128), np.float32),
                invD16c=np.full((128, 1), 1.0 / D, np.float16),
                esel_in=esel_np,
                cjmb_in=tile(pos256 % 4 - BIG), cj_in=tile(pos256 % 4),
                cgmb_in=tile(pos64 % 4 - BIG), cg_in=tile(pos64 % 4),
                cemb_in=tile(pos256 % 16 - BIG), ce_in=tile(pos256 % 16))


def kernel(L0_attn_x, L0_attn_Wi, L0_attn_bi, L0_attn_Wh, L0_attn_bh,
           L0_ffn_gate_weight, L0_ffn_e_bias, L0_ffn_gate_proj, L0_ffn_up_proj,
           L0_ffn_down_proj):
    x = np.asarray(L0_attn_x, np.float32)
    shared = dict(
        wiT=np.ascontiguousarray(np.asarray(L0_attn_Wi, np.float32).T),
        whT=np.ascontiguousarray(np.asarray(L0_attn_Wh, np.float32).T),
        b_in=np.ascontiguousarray(np.stack([np.asarray(L0_attn_bi, np.float32),
                                            np.asarray(L0_attn_bh, np.float32)])),
        gwT=np.ascontiguousarray(np.asarray(L0_ffn_gate_weight, np.float32).T),
        ebias=np.ascontiguousarray(np.asarray(L0_ffn_e_bias, np.float32)[None, :]),
        gpT=np.ascontiguousarray(np.asarray(L0_ffn_gate_proj, np.float32).transpose(0, 2, 1)),
        upT=np.ascontiguousarray(np.asarray(L0_ffn_up_proj, np.float32).transpose(0, 2, 1)),
        dpT=np.ascontiguousarray(np.asarray(L0_ffn_down_proj, np.float32).transpose(0, 2, 1)),
        **_host_consts(),
    )
    try:
        from concourse.compiler_utils import get_compiler_flags, set_compiler_flags
        set_compiler_flags([f.replace("--enable-ldw-opt=false", "--enable-ldw-opt=true")
                            for f in get_compiler_flags()])
    except Exception:
        pass
    nc = build_nc()
    in_maps = [dict(x_fm=np.ascontiguousarray(x[c].T), **shared) for c in range(8)]
    res = run_bass_kernel_spmd(nc, in_maps, core_ids=list(range(8)))
    out = np.stack([res.results[c]["outT"] for c in range(8)])
    return np.ascontiguousarray(out.transpose(0, 2, 1))
